# version 05
# baseline (speedup 1.0000x reference)
import math
import numpy as np
import concourse.bass as bass
import concourse.mybir as mybir
from concourse.bass_utils import run_bass_kernel_spmd

F32 = mybir.dt.float32
BF16 = mybir.dt.bfloat16
AF = mybir.ActivationFunctionType
ALU = mybir.AluOpType
AX = mybir.AxisListType

B, S, E = 2, 4096, 4096
NI = 32
NH = 8
INVSQ_DK = float(1.0 / np.sqrt(np.float32(32.0)))
LN_EPS = 1e-5
EXP_SHIFT = -10.0

TB_CCK = 0
TB_SSK = 4096
TB_CCQ = 8192
TB_SSQ = 9216
TB_MSK = 10240
WRK = 12288
BA_COLS = 67584


def build_nc(phases=8, debug=False):
    nc = bass.Bass("TRN2", target_bir_lowering=False)
    dt = nc.dram_tensor

    xt = dt("xt", [8, 128, NI, 512], F32, kind="ExternalInput")
    wq = dt("wq", [32, 128, 4096], F32, kind="ExternalInput")
    wk = dt("wk", [8, 128, 4096], F32, kind="ExternalInput")
    wv = dt("wv", [8, 128, 4096], F32, kind="ExternalInput")
    wo = dt("wo", [32, 128, 4096], F32, kind="ExternalInput")
    wf = dt("wf", [32, 128, 4096], F32, kind="ExternalInput")
    gwqk_i = dt("gwqk_i", [32, 128, 512], F32, kind="ExternalInput")
    gwat_i = dt("gwat_i", [32, 128, 128], F32, kind="ExternalInput")
    bmt_i = dt("bmt_i", [128, 4096], F32, kind="ExternalInput")
    bcol_i = dt("bcol_i", [128, 32], F32, kind="ExternalInput")
    cck_i = dt("cck_i", [128, 4096], F32, kind="ExternalInput")
    ssk_i = dt("ssk_i", [128, 4096], F32, kind="ExternalInput")
    ccq_i = dt("ccq_i", [128, 1024], F32, kind="ExternalInput")
    ssq_i = dt("ssq_i", [128, 1024], F32, kind="ExternalInput")
    msk_i = dt("msk_i", [128, 2048], F32, kind="ExternalInput")
    eye_i = dt("eye_i", [128, 128], F32, kind="ExternalInput")
    out = dt("out", [128, 32768], F32, kind="ExternalOutput")

    dkind = "ExternalOutput" if debug else "Internal"
    oxln = dt("oxln", [128, 32768], BF16, kind=dkind)
    kd = dt("kd", [128, 32768], BF16, kind=dkind)
    vd = dt("vd", [128, 32768], BF16, kind=dkind)
    qd = dt("qd", [128, 32768], BF16, kind=dkind)
    aod = dt("aod", [128, 32768], BF16, kind=dkind)
    x2d = dt("x2d", [128, 32768], F32, kind=dkind)
    xl2d = dt("xl2d", [128, 32768], BF16, kind=dkind)
    gtd = dt("gtd", [128, 1024], BF16, kind=dkind)
    qkd = dt("qkd", [128, 640], F32, kind=dkind)
    gsd = dt("gsd", [128, 32], F32, kind=dkind)

    ba = nc.alloc_sbuf_tensor("ba", [128, BA_COLS], BF16)
    fa = nc.alloc_sbuf_tensor("fa", [128, 4608], F32)
    ones_c = nc.alloc_sbuf_tensor("ones_c", [128, 1], BF16)
    ones_cf = nc.alloc_sbuf_tensor("ones_cf", [128, 1], F32)
    ones_r = nc.alloc_sbuf_tensor("ones_r", [1, 128], BF16)
    epsc = nc.alloc_sbuf_tensor("epsc", [1, 1], F32)
    expb = nc.alloc_sbuf_tensor("expb", [128, 1], F32)
    bcol_sb = nc.alloc_sbuf_tensor("bcol_sb", [128, 32], F32)
    eyebf = nc.alloc_sbuf_tensor("eyebf", [128, 128], BF16)
    rows = nc.alloc_sbuf_tensor("rows", [1, 2048], F32)
    rows2 = nc.alloc_sbuf_tensor("rows2", [1, 2048], F32)
    rowsb = nc.alloc_sbuf_tensor("rowsb", [1, 2048], BF16)
    bcr = nc.alloc_sbuf_tensor("bcr", [128, 512], F32)
    bcmr = nc.alloc_sbuf_tensor("bcmr", [128, 512], F32)
    bcr2 = nc.alloc_sbuf_tensor("bcr2", [128, 1024], F32)
    bcmr2 = nc.alloc_sbuf_tensor("bcmr2", [128, 1024], F32)
    recb = nc.alloc_sbuf_tensor("recb", [128, 512], F32)
    gsml = nc.alloc_sbuf_tensor("gsml", [128, 32], F32)
    qsb = nc.alloc_sbuf_tensor("qsb", [128, 512], F32)
    hsb = nc.alloc_sbuf_tensor("hsb", [128, 128], F32)
    s8b = nc.alloc_sbuf_tensor("s8b", [128, 256], F32)
    gbf = nc.alloc_sbuf_tensor("gbf", [128, 128], BF16)
    gt_sb = nc.alloc_sbuf_tensor("gt_sb", [128, 1024], BF16)

    psA = nc.alloc_psum_tensor("psA", [128, 2048], F32).ap()
    pbE = nc.alloc_psum_tensor("pbE", [128, 512], F32).ap()
    pbF = nc.alloc_psum_tensor("pbF", [128, 512], F32).ap()
    pbD = nc.alloc_psum_tensor("pbD", [128, 512], F32).ap()
    psTb = nc.alloc_psum_tensor("psTb", [128, 1024], BF16).ap()

    def BA(a, n):
        return ba.ap()[:, a:a + n]

    def FA(a, n):
        return fa.ap()[:, a:a + n]

    class Sem:
        def __init__(self, name):
            self.h = nc.semaphore(name).__enter__()
            self.n = 0

        def inc(self, instr, k):
            instr.then_inc(self.h, k)
            self.n += k
            return self.n

    s_sl = Sem("s_sl")
    s_ss = Sem("s_ss")
    s_gl = Sem("s_gl")
    s_te = Sem("s_te")
    s_ac = Sem("s_ac")
    s_dv = Sem("s_dv")
    ALLS = [s_sl, s_ss, s_gl, s_te, s_ac, s_dv]

    block = nc.Block().__enter__()

    def T(fn):
        block.tensor(fn)

    def A(fn):
        block.scalar(fn)

    def V(fn):
        block.vector(fn)

    def GP(fn):
        block.gpsimd(fn)

    def SY(fn):
        block.sync(fn)

    def w(eng, sem, n):
        if n > 0:
            eng.wait_ge(sem.h, n)

    def fence():
        snap = [(s, s.n) for s in ALLS]

        def mk(eng):
            for s, n in snap:
                w(eng, s, n)
        T(mk)
        V(mk)
        A(mk)
        GP(mk)
        SY(mk)

    def _(g):
        g.dma_start(BA(TB_CCK, 4096), cck_i.ap()).then_inc(s_gl.h, 16)
        g.dma_start(BA(TB_SSK, 4096), ssk_i.ap()).then_inc(s_gl.h, 16)
        g.dma_start(BA(TB_CCQ, 1024), ccq_i.ap()).then_inc(s_gl.h, 16)
        g.dma_start(BA(TB_SSQ, 1024), ssq_i.ap()).then_inc(s_gl.h, 16)
        g.dma_start(BA(TB_MSK, 2048), msk_i.ap()).then_inc(s_gl.h, 16)
        g.dma_start(eyebf.ap(), eye_i.ap()).then_inc(s_gl.h, 16)
        s_gl.n += 96
    GP(_)

    def _(sy):
        sy.dma_start(bcol_sb.ap(), bcol_i.ap()).then_inc(s_sl.h, 16)
        s_sl.n += 16
    SY(_)

    def _(v):
        v.memset(ones_c.ap(), 1.0)
        v.memset(ones_cf.ap(), 1.0)
        v.memset(ones_r.ap(), 1.0)
        v.memset(epsc.ap(), LN_EPS)
        ins = v.memset(expb.ap(), EXP_SHIFT)
        s_dv.inc(ins, 1)
    V(_)
    fence()

    XG = [WRK, WRK + 16384]
    WKB = [WRK + 32768, WRK + 36864]
    WVB = [WRK + 40960, WRK + 45056]
    KEV = [WRK + 49152, WRK + 49664]
    VEVF = [WRK + 50176, WRK + 50688]
    VEVT = [WRK + 51200, WRK + 51712]
    SQB = [WRK + 52224, WRK + 52736]

    def xgv(par, i=None):
        if i is None:
            return BA(XG[par], 16384)
        return BA(XG[par] + i * 512, 512)

    st = dict(xg_te=[0, 0], xg_ss=[0, 0], kev_ss=[0, 0], vevf_ac=[0, 0],
              vevt_ss=[0, 0], kv_dv=0, vf_te=[0, 0], pstb_ac=[0, 0], swp_dv=0,
              wk_te=[0, 0], bc_ac=0, fin_dv=0, last_norm_dv=0)

    for pair in range(4):
        glist = [2 * pair, 2 * pair + 1]
        norm_dv_g = [0, 0]
        for par, g in enumerate(glist):
            def _(gp, g=g, par=par):
                w(gp, s_te, st["xg_te"][par])
                w(gp, s_ss, st["xg_ss"][par])
                ins = gp.dma_start(xgv(par), xt.ap()[g].rearrange("p i t -> p (i t)"))
                s_gl.inc(ins, 16)
            GP(_)
            xg_rdy = s_gl.n

            te0 = s_te.n
            ac0 = s_ac.n
            for i in range(NI):
                def _(a, i=i, par=par, xr=xg_rdy, te0=te0):
                    if i == 0:
                        w(a, s_gl, xr)
                    if i >= 2:
                        w(a, s_te, te0 + i - 1)
                    ins = a.activation(BA(SQB[i % 2], 512), xgv(par, i), AF.Square)
                    s_ac.inc(ins, 1)
                A(_)

                def _(t, i=i, par=par, xr=xg_rdy, ac0=ac0, fp=st["fin_dv"]):
                    if i == 0:
                        w(t, s_gl, xr)
                        w(t, s_dv, fp)
                    t.matmul(psA[0:1, 0:512], ones_c.ap(), xgv(par, i),
                             start=(i == 0), stop=(i == NI - 1))
                    w(t, s_ac, ac0 + i + 1)
                    ins = t.matmul(psA[0:1, 512:1024], ones_c.ap(), BA(SQB[i % 2], 512),
                                   start=(i == 0), stop=(i == NI - 1))
                    s_te.inc(ins, 1)
                T(_)
            stats_te = s_te.n

            def _(v, sd=stats_te):
                w(v, s_te, sd)
                v.tensor_scalar(rows.ap()[:, 0:512], psA[0:1, 0:512],
                                1.0 / 4096.0, None, ALU.mult)
                v.tensor_scalar(rows.ap()[:, 1536:2048], psA[0:1, 512:1024],
                                1.0 / 4096.0, None, ALU.mult)
                v.tensor_tensor(rows.ap()[:, 512:1024], rows.ap()[:, 0:512],
                                rows.ap()[:, 0:512], ALU.mult)
                ins = v.tensor_tensor(rows.ap()[:, 1536:2048], rows.ap()[:, 1536:2048],
                                      rows.ap()[:, 512:1024], ALU.subtract)
                s_dv.inc(ins, 1)
            V(_)
            st["fin_dv"] = s_dv.n

            def _(a, vd_=s_dv.n):
                w(a, s_dv, vd_)
                ins = a.activation(rows.ap()[:, 512:1024], rows.ap()[:, 1536:2048],
                                   AF.Sqrt, bias=epsc.ap())
                s_ac.inc(ins, 1)
            A(_)
            sd_ac = s_ac.n

            def _(v, sd=sd_ac):
                w(v, s_ac, sd)
                v.reciprocal(rows.ap()[:, 512:1024], rows.ap()[:, 512:1024])
                v.tensor_tensor(rows.ap()[:, 1024:1536], rows.ap()[:, 0:512],
                                rows.ap()[:, 512:1024], ALU.mult)
                v.tensor_copy(rowsb.ap()[:, 0:512], rows.ap()[:, 512:1024])
                ins = v.tensor_copy(rowsb.ap()[:, 512:1024], rows.ap()[:, 1024:1536])
                s_dv.inc(ins, 1)
            V(_)
            rows_dv = s_dv.n

            def _(t, rd=rows_dv, ba_=st["bc_ac"]):
                w(t, s_dv, rd)
                w(t, s_ac, ba_)
                t.matmul(psA[:, 1024:1536], ones_r.ap(), rowsb.ap()[:, 0:512],
                         start=True, stop=True)
                ins = t.matmul(psA[:, 1536:2048], ones_r.ap(), rowsb.ap()[:, 512:1024],
                               start=True, stop=True)
                s_te.inc(ins, 1)
            T(_)
            bc_te = s_te.n

            def _(a, bt=bc_te, nd=st["last_norm_dv"]):
                w(a, s_te, bt)
                w(a, s_dv, nd)
                a.copy(bcr.ap(), psA[:, 1024:1536])
                ins = a.copy(bcmr.ap(), psA[:, 1536:2048])
                s_ac.inc(ins, 1)
            A(_)
            st["bc_ac"] = s_ac.n
            bc_done = s_ac.n

            def _(v, par=par, bd=bc_done):
                w(v, s_ac, bd)
                for i in range(NI):
                    v.tensor_tensor(xgv(par, i), xgv(par, i), bcr.ap(), ALU.mult)
                    ins = v.tensor_tensor(xgv(par, i), xgv(par, i), bcmr.ap(), ALU.subtract)
                s_dv.inc(ins, 1)
            V(_)
            norm_dv_g[par] = s_dv.n
            st["last_norm_dv"] = s_dv.n

            def _(sy, g=g, par=par, nd=s_dv.n):
                w(sy, s_dv, nd)
                dst = oxln.ap().rearrange("p (i g q) -> p i g q", g=8, q=128)[:, :, g, :]
                src = BA(XG[par], 16384).rearrange("p (i t) -> p i t", t=512)[:, :, 0:128]
                ins = sy.dma_start(dst, src)
                st["xg_ss"][par] = s_ss.inc(ins, 16)
            SY(_)

        for h in range(NH):
            hp = h % 2

            def _(gp, h=h, hp=hp):
                w(gp, s_te, st["wk_te"][hp])
                gp.dma_start(BA(WKB[hp], 4096), wk.ap()[h]).then_inc(s_gl.h, 16)
                ins = gp.dma_start(BA(WVB[hp], 4096), wv.ap()[h])
                s_gl.n += 16
                s_gl.inc(ins, 16)
            GP(_)
            wkv_rdy = s_gl.n

            for par, g in enumerate(glist):
                def _(t, hp=hp, par=par, wr=wkv_rdy, nd=norm_dv_g[par], kv=st["kv_dv"]):
                    w(t, s_gl, wr)
                    w(t, s_dv, nd)
                    w(t, s_dv, kv)
                    for i in range(NI):
                        ins = t.matmul(pbE[:, :], BA(WKB[hp] + i * 128, 128), xgv(par, i),
                                       start=(i == 0), stop=(i == NI - 1))
                    s_te.inc(ins, 1)
                T(_)
                k_te = s_te.n

                def _(a, kt=k_te, sd=st["swp_dv"]):
                    w(a, s_te, kt)
                    w(a, s_dv, sd)
                    a.copy(FA(0, 512)[0:64, :], pbE[64:128, :])
                    ins = a.copy(FA(0, 512)[64:128, :], pbE[0:64, :])
                    s_ac.inc(ins, 1)
                A(_)
                swp_ac = s_ac.n

                def _(v, g=g, par=par, sa=swp_ac, ks=st["kev_ss"][par]):
                    w(v, s_ac, sa)
                    w(v, s_ss, ks)
                    v.tensor_tensor(FA(512, 512), pbE[:, :], BA(TB_CCK + g * 512, 512), ALU.mult)
                    v.tensor_tensor(FA(0, 512), FA(0, 512), BA(TB_SSK + g * 512, 512), ALU.mult)
                    ins = v.tensor_tensor(BA(KEV[par], 512), FA(512, 512), FA(0, 512), ALU.add)
                    st["kv_dv"] = s_dv.inc(ins, 1)
                    st["swp_dv"] = st["kv_dv"]
                V(_)

                def _(sy, h=h, g=g, par=par, kdv=st["kv_dv"]):
                    w(sy, s_dv, kdv)
                    ins = sy.dma_start(kd.ap()[:, h * 4096 + g * 512: h * 4096 + (g + 1) * 512],
                                       BA(KEV[par], 512))
                    st["kev_ss"][par] = s_ss.inc(ins, 16)
                SY(_)

                def _(t, hp=hp, par=par, va=st["vevf_ac"][par]):
                    w(t, s_ac, va)
                    for i in range(NI):
                        ins = t.matmul(pbF[:, :], BA(WVB[hp] + i * 128, 128), xgv(par, i),
                                       start=(i == 0), stop=(i == NI - 1))
                    s_te.inc(ins, 1)
                T(_)
                v_te = s_te.n

                def _(a, vt=v_te, par=par, ft=st["vf_te"][par]):
                    w(a, s_te, vt)
                    w(a, s_te, ft)
                    ins = a.copy(BA(VEVF[par], 512), pbF[:, :])
                    st["vevf_ac"][par] = s_ac.inc(ins, 1)
                A(_)
                vf_ac = s_ac.n

                def _(t, par=par, va=vf_ac, pa=st["pstb_ac"][par]):
                    w(t, s_ac, va)
                    w(t, s_ac, pa)
                    for ts in range(4):
                        ins = t.transpose(psTb[:, par * 512 + ts * 128: par * 512 + (ts + 1) * 128],
                                          BA(VEVF[par] + ts * 128, 128), eyebf.ap())
                    st["vf_te"][par] = s_te.inc(ins, 1)
                T(_)
                tr_te = s_te.n

                def _(a, par=par, tt=tr_te, vs=st["vevt_ss"][par]):
                    w(a, s_te, tt)
                    w(a, s_ss, vs)
                    ins = a.copy(BA(VEVT[par], 512), psTb[:, par * 512:(par + 1) * 512])
                    st["pstb_ac"][par] = s_ac.inc(ins, 1)
                A(_)
                vt_ac = s_ac.n

                def _(sy, h=h, g=g, par=par, va=vt_ac):
                    w(sy, s_ac, va)
                    ins = sy.dma_start(vd.ap()[:, h * 4096 + g * 512: h * 4096 + (g + 1) * 512],
                                       BA(VEVT[par], 512))
                    st["vevt_ss"][par] = s_ss.inc(ins, 16)
                SY(_)
            st["wk_te"][hp] = s_te.n
        st["xg_te"][0] = s_te.n
        st["xg_te"][1] = s_te.n

    fence()
    if phases <= 1:
        def _(sy):
            sy.wait_ge(s_ss.h, s_ss.n)
        SY(_)
        block.__exit__(None, None, None)
        return nc

    OXB = WRK
    WQB = [WRK + 32768, WRK + 36864]
    QEV = [WRK + 40960, WRK + 41472, WRK + 41984, WRK + 42496]

    def _(sy):
        ins = sy.dma_start(BA(OXB, 32768), oxln.ap())
        s_sl.inc(ins, 16)
    SY(_)
    ox_rdy = s_sl.n

    p2 = dict(wq_te=[0, 0], q_dv=[0, 0], swp_dv=0, qev_ss=[0, 0, 0, 0])
    for f in range(32):
        fp = f % 2
        h, g4 = f // 4, f % 4

        def _(gp, f=f, fp=fp):
            w(gp, s_te, p2["wq_te"][fp])
            ins = gp.dma_start(BA(WQB[fp], 4096), wq.ap()[f])
            s_gl.inc(ins, 16)
        GP(_)
        wq_rdy = s_gl.n

        for half in range(2):
            def _(t, f=f, fp=fp, half=half, wr=wq_rdy, orr=ox_rdy, qv=p2["q_dv"][half]):
                pb = pbE if half == 0 else pbF
                if f == 0 and half == 0:
                    w(t, s_sl, orr)
                w(t, s_gl, wr)
                w(t, s_dv, qv)
                for i in range(NI):
                    ins = t.matmul(pb[:, :], BA(WQB[fp] + i * 128, 128),
                                   BA(OXB + i * 1024 + half * 512, 512),
                                   start=(i == 0), stop=(i == NI - 1))
                if half == 1:
                    p2["wq_te"][fp] = s_te.n + 1
                s_te.inc(ins, 1)
            T(_)
            q_te = s_te.n

            def _(a, qt=q_te, half=half, sd=p2["swp_dv"]):
                pb = pbE if half == 0 else pbF
                w(a, s_te, qt)
                w(a, s_dv, sd)
                a.copy(FA(0, 512)[0:64, :], pb[64:128, :])
                ins = a.copy(FA(0, 512)[64:128, :], pb[0:64, :])
                s_ac.inc(ins, 1)
            A(_)
            swp_ac = s_ac.n

            qe = 2 * fp + half

            def _(v, half=half, sa=swp_ac, qe=qe, qs=p2["qev_ss"][qe]):
                pb = pbE if half == 0 else pbF
                w(v, s_ac, sa)
                w(v, s_ss, qs)
                v.tensor_tensor(FA(512, 512), pb[:, :], BA(TB_CCQ + half * 512, 512), ALU.mult)
                v.tensor_tensor(FA(0, 512), FA(0, 512), BA(TB_SSQ + half * 512, 512), ALU.mult)
                ins = v.tensor_tensor(BA(QEV[qe], 512), FA(512, 512), FA(0, 512), ALU.add)
                p2["q_dv"][half] = s_dv.inc(ins, 1)
                p2["swp_dv"] = p2["q_dv"][half]
            V(_)

            def _(sy, h=h, g4=g4, half=half, qe=qe, qdv=p2["q_dv"][half]):
                w(sy, s_dv, qdv)
                dst = qd.ap().rearrange("p (hh j g q) -> p hh j g q", hh=8, j=8, g=4)[
                    :, h, 4 * half:4 * half + 4, g4, :]
                src = BA(QEV[qe], 512).rearrange("p (j q) -> p j q", j=4)
                ins = sy.dma_start(dst, src)
                p2["qev_ss"][qe] = s_ss.inc(ins, 16)
            SY(_)

    fence()
    if phases <= 2:
        def _(sy):
            sy.wait_ge(s_ss.h, s_ss.n)
        SY(_)
        block.__exit__(None, None, None)
        return nc

    KHB = [WRK, WRK + 4096]
    VBF = [WRK + 8192, WRK + 12288]
    QTB = [WRK + 16384, WRK + 16896]
    PRG = [WRK + 17408, WRK + 19456]
    AO = WRK + 21504

    a3 = dict(kv_te=[0, 0], qt_te=[0, 0], prg_te=[0, 0],
              pbO_dv=[0, 0], den_dv=0, rb_ac=0,
              psA_ac_0=0, psA_ac_1=0)

    for h in range(NH):
        par = h % 2

        def _(sy, h=h, par=par, kt=a3["kv_te"][par]):
            w(sy, s_te, kt)
            sy.dma_start(BA(KHB[par], 4096),
                         kd.ap()[:, h * 4096:(h + 1) * 4096]).then_inc(s_sl.h, 16)
            ins = sy.dma_start(BA(VBF[par], 4096),
                               vd.ap()[:, h * 4096:(h + 1) * 4096])
            s_sl.n += 16
            s_sl.inc(ins, 16)
        SY(_)
        kv_rdy = s_sl.n

        for j in range(NH):
            jp = j % 2

            def _(sy, h=h, j=j, jp=jp, qt=a3["qt_te"][jp]):
                w(sy, s_te, qt)
                ins = sy.dma_start(BA(QTB[jp], 512),
                                   qd.ap()[:, h * 4096 + j * 512: h * 4096 + (j + 1) * 512])
                s_sl.inc(ins, 16)
            SY(_)
            qt_rdy = s_sl.n

            for c2 in range(j + 1):
                pp = c2 % 2

                for bk in range(2):
                    def _(t, j=j, jp=jp, c2=c2, par=par, bk=bk, qr=qt_rdy, kr=kv_rdy,
                          pa=a3["psA_ac_%d" % bk], po=a3["pbO_dv"][jp]):
                        if c2 == 0 and bk == 0:
                            w(t, s_sl, qr)
                            w(t, s_sl, kr)
                            w(t, s_dv, po)
                        w(t, s_ac, pa)
                        for rr in (0, 1):
                            kt = c2 * 4 + 2 * bk + rr
                            ins = t.matmul(psA[:, (2 * bk + rr) * 512:(2 * bk + rr + 1) * 512],
                                           BA(KHB[par] + kt * 128, 128), BA(QTB[jp], 512),
                                           start=True, stop=True)
                        s_te.inc(ins, 1)
                    T(_)
                    sc_te = s_te.n

                    def _(a, pp=pp, bk=bk, st_=sc_te, pt=a3["prg_te"][pp]):
                        w(a, s_te, st_)
                        if bk == 0:
                            w(a, s_te, pt)
                        ins = a.activation(BA(PRG[pp] + bk * 1024, 1024),
                                           psA[:, bk * 1024:(bk + 1) * 1024],
                                           AF.Exp, bias=expb.ap())
                        a3["psA_ac_%d" % bk] = s_ac.inc(ins, 1)
                    A(_)
                exp_ac = s_ac.n

                mask_dv = 0
                if c2 == j:
                    def _(v, pp=pp, ea=exp_ac):
                        w(v, s_ac, ea)
                        ins = v.tensor_tensor(BA(PRG[pp], 2048), BA(PRG[pp], 2048),
                                              BA(TB_MSK, 2048), ALU.mult)
                        s_dv.inc(ins, 1)
                    V(_)
                    mask_dv = s_dv.n

                def _(t, j=j, jp=jp, c2=c2, par=par, pp=pp, ea=exp_ac, md=mask_dv,
                      dd=a3["den_dv"]):
                    pbO = pbE if jp == 0 else pbF
                    if md:
                        w(t, s_dv, md)
                    else:
                        w(t, s_ac, ea)
                    if c2 == 0:
                        w(t, s_dv, dd)
                    for rr in range(4):
                        kt = c2 * 4 + rr
                        t.matmul(pbD[0:1, :], ones_c.ap(), BA(PRG[pp] + rr * 512, 512),
                                 start=(c2 == 0 and rr == 0), stop=(c2 == j and rr == 3))
                        ins = t.matmul(pbO[:, :], BA(VBF[par] + kt * 128, 128),
                                       BA(PRG[pp] + rr * 512, 512),
                                       start=(c2 == 0 and rr == 0), stop=(c2 == j and rr == 3))
                    a3["prg_te"][pp] = s_te.inc(ins, 1)
                T(_)
            den_te = s_te.n
            a3["qt_te"][jp] = s_te.n

            def _(v, dt_=den_te, ra=a3["rb_ac"]):
                w(v, s_te, dt_)
                w(v, s_ac, ra)
                v.reciprocal(rows.ap()[:, 0:512], pbD[0:1, :])
                ins = v.tensor_copy(rowsb.ap()[:, 0:512], rows.ap()[:, 0:512])
                a3["den_dv"] = s_dv.inc(ins, 1)
            V(_)
            den_dv = s_dv.n

            def _(t, dd=den_dv, pa=a3["psA_ac_0"]):
                w(t, s_dv, dd)
                w(t, s_ac, pa)
                ins = t.matmul(psA[:, 0:512], ones_r.ap(), rowsb.ap()[:, 0:512],
                               start=True, stop=True)
                s_te.inc(ins, 1)
            T(_)
            rb_te = s_te.n

            def _(a, rt=rb_te, ao0=a3["pbO_dv"][0], ao1=a3["pbO_dv"][1]):
                w(a, s_te, rt)
                w(a, s_dv, max(ao0, ao1))
                ins = a.copy(recb.ap(), psA[:, 0:512])
                a3["rb_ac"] = s_ac.inc(ins, 1)
                a3["psA_ac_0"] = a3["rb_ac"]
            A(_)
            rb_ac = s_ac.n

            def _(v, h=h, j=j, jp=jp, ra=rb_ac):
                pbO = pbE if jp == 0 else pbF
                w(v, s_ac, ra)
                dst = BA(AO, 32768).rearrange("p (e u) -> p e u", u=1024)[
                    :, 4 * h:4 * h + 4, j * 128:(j + 1) * 128]
                ins = v.tensor_tensor(dst, pbO.rearrange("p (g q) -> p g q", g=4),
                                      recb.ap().rearrange("p (g q) -> p g q", g=4), ALU.mult)
                a3["pbO_dv"][jp] = s_dv.inc(ins, 1)
            V(_)
        a3["kv_te"][par] = s_te.n
    ao_dv = s_dv.n

    if debug:
        def _(sy, ad=ao_dv):
            w(sy, s_dv, ad)
            ins = sy.dma_start(aod.ap(), BA(AO, 32768))
            s_ss.inc(ins, 16)
        SY(_)

    fence()
    if phases <= 3:
        def _(sy):
            sy.wait_ge(s_ss.h, s_ss.n)
        SY(_)
        block.__exit__(None, None, None)
        return nc

    WOB = [WRK, WRK + 4096]
    XRS = [0, 512]
    X2B = [1024, 1536]
    SQ2 = [2048, 2560]

    p5 = dict(wo_te=[0, 0], x2_dv=[0, 0], st_te=[0, 0], x2_ss=[0, 0])
    for o in range(32):
        op = o % 2

        def _(gp, o=o, op=op, wt_=p5["wo_te"][op]):
            w(gp, s_te, wt_)
            ins = gp.dma_start(BA(WOB[op], 4096), wo.ap()[o])
            s_gl.inc(ins, 16)
        GP(_)
        wo_rdy = s_gl.n

        for tc in range(2):
            def _(sy, o=o, tc=tc, xd=p5["x2_dv"][tc]):
                w(sy, s_dv, xd)
                src = xt.ap().rearrange("g p i t -> p g i t")[:, 4 * tc:4 * tc + 4, o, 0:128]
                ins = sy.dma_start(FA(XRS[tc], 512).rearrange("p (g q) -> p g q", g=4), src)
                p5["xr_sl_%d" % tc] = s_sl.inc(ins, 16)
            SY(_)

            def _(t, o=o, op=op, tc=tc, wr=wo_rdy, xv=p5["x2_dv"][tc]):
                pb = pbE if tc == 0 else pbF
                w(t, s_gl, wr)
                w(t, s_dv, xv)
                for e in range(NI):
                    ins = t.matmul(pb[:, :], BA(WOB[op] + e * 128, 128),
                                   BA(AO + e * 1024 + tc * 512, 512),
                                   start=(e == 0), stop=(e == NI - 1))
                if tc == 1:
                    p5["wo_te"][op] = s_te.n + 1
                s_te.inc(ins, 1)
            T(_)
            y_te = s_te.n

            def _(v, tc=tc, yt=y_te, xr=p5["xr_sl_%d" % tc], xs=p5["x2_ss"][tc],
                  sq=p5["st_te"][tc]):
                pb = pbE if tc == 0 else pbF
                w(v, s_te, yt)
                w(v, s_sl, xr)
                w(v, s_ss, xs)
                w(v, s_te, sq)
                v.tensor_tensor(FA(X2B[tc], 512), pb[:, :], FA(XRS[tc], 512), ALU.add)
                ins = v.tensor_tensor(FA(SQ2[tc], 512), FA(X2B[tc], 512),
                                      FA(X2B[tc], 512), ALU.mult)
                p5["x2_dv"][tc] = s_dv.inc(ins, 1)
            V(_)
            x2_dv = s_dv.n

            def _(sy, o=o, tc=tc, xd=x2_dv):
                w(sy, s_dv, xd)
                ins = sy.dma_start(x2d.ap()[:, o * 1024 + tc * 512: o * 1024 + (tc + 1) * 512],
                                   FA(X2B[tc], 512))
                p5["x2_ss"][tc] = s_ss.inc(ins, 16)
            SY(_)

            def _(t, o=o, tc=tc, xd=x2_dv):
                w(t, s_dv, xd)
                t.matmul(psA[0:1, tc * 512:(tc + 1) * 512], ones_cf.ap(), FA(X2B[tc], 512),
                         start=(o == 0), stop=(o == 31))
                ins = t.matmul(psA[0:1, 1024 + tc * 512:1024 + (tc + 1) * 512], ones_cf.ap(),
                               FA(SQ2[tc], 512), start=(o == 0), stop=(o == 31))
                p5["st_te"][tc] = s_te.inc(ins, 1)
            T(_)
    st5_te = s_te.n

    def _(v, sd=st5_te):
        w(v, s_te, sd)
        v.tensor_scalar(rows2.ap()[:, 0:1024], psA[0:1, 0:1024], 1.0 / 4096.0, None, ALU.mult)
        v.tensor_scalar(rows.ap()[:, 1024:2048], psA[0:1, 1024:2048], 1.0 / 4096.0, None, ALU.mult)
        v.tensor_tensor(rows2.ap()[:, 1024:2048], rows2.ap()[:, 0:1024],
                        rows2.ap()[:, 0:1024], ALU.mult)
        ins = v.tensor_tensor(rows.ap()[:, 1024:2048], rows.ap()[:, 1024:2048],
                              rows2.ap()[:, 1024:2048], ALU.subtract)
        s_dv.inc(ins, 1)
    V(_)
    v2_dv = s_dv.n

    def _(a, vd_=v2_dv):
        w(a, s_dv, vd_)
        ins = a.activation(rows2.ap()[:, 1024:2048], rows.ap()[:, 1024:2048],
                           AF.Sqrt, bias=epsc.ap())
        s_ac.inc(ins, 1)
    A(_)
    sd2_ac = s_ac.n

    def _(v, sd=sd2_ac):
        w(v, s_ac, sd)
        v.reciprocal(rows2.ap()[:, 1024:2048], rows2.ap()[:, 1024:2048])
        v.tensor_tensor(rows.ap()[:, 0:1024], rows2.ap()[:, 0:1024],
                        rows2.ap()[:, 1024:2048], ALU.mult)
        v.tensor_copy(rowsb.ap()[:, 0:1024], rows2.ap()[:, 1024:2048])
        ins = v.tensor_copy(rowsb.ap()[:, 1024:2048], rows.ap()[:, 0:1024])
        s_dv.inc(ins, 1)
    V(_)
    r2_dv = s_dv.n

    def _(t, rd=r2_dv):
        w(t, s_dv, rd)
        t.matmul(psA[:, 0:512], ones_r.ap(), rowsb.ap()[:, 0:512], start=True, stop=True)
        t.matmul(psA[:, 512:1024], ones_r.ap(), rowsb.ap()[:, 512:1024], start=True, stop=True)
        t.matmul(psA[:, 1024:1536], ones_r.ap(), rowsb.ap()[:, 1024:1536], start=True, stop=True)
        ins = t.matmul(psA[:, 1536:2048], ones_r.ap(), rowsb.ap()[:, 1536:2048],
                       start=True, stop=True)
        s_te.inc(ins, 1)
    T(_)
    bc2_te = s_te.n

    def _(a, bt=bc2_te):
        w(a, s_te, bt)
        a.copy(bcr2.ap(), psA[:, 0:1024])
        ins = a.copy(bcmr2.ap(), psA[:, 1024:2048])
        s_ac.inc(ins, 1)
    A(_)
    fence()
    if phases <= 5:
        def _(sy):
            sy.wait_ge(s_ss.h, s_ss.n)
        SY(_)
        block.__exit__(None, None, None)
        return nc

    XL2 = WRK + 2048
    X2S = [3072, 3584]

    p6 = dict(xl_dv=[0, 0])
    for i in range(NI):
        for tc in range(2):
            pp = (2 * i + tc) % 2

            def _(sy, i=i, tc=tc, pp=pp, xd=p6["xl_dv"][pp]):
                w(sy, s_dv, xd)
                ins = sy.dma_start(FA(X2S[pp], 512),
                                   x2d.ap()[:, i * 1024 + tc * 512: i * 1024 + (tc + 1) * 512])
                s_sl.inc(ins, 16)
            SY(_)
            x2_rdy = s_sl.n

            def _(v, i=i, tc=tc, pp=pp, xr=x2_rdy):
                w(v, s_sl, xr)
                v.tensor_tensor(FA(X2S[pp], 512), FA(X2S[pp], 512),
                                bcr2.ap()[:, tc * 512:(tc + 1) * 512], ALU.mult)
                ins = v.tensor_tensor(BA(XL2 + i * 1024 + tc * 512, 512), FA(X2S[pp], 512),
                                      bcmr2.ap()[:, tc * 512:(tc + 1) * 512], ALU.subtract)
                p6["xl_dv"][pp] = s_dv.inc(ins, 1)
            V(_)
    xl2_dv = s_dv.n

    if debug:
        def _(sy, xd=xl2_dv):
            w(sy, s_dv, xd)
            ins = sy.dma_start(xl2d.ap(), BA(XL2, 32768))
            s_ss.inc(ins, 16)
        SY(_)
    fence()

    GWQ = WRK + 34816
    GWA = WRK + 51200

    def _(gp):
        gp.dma_start(BA(GWQ, 16384).rearrange("p (i n) -> p i n", n=512),
                     gwqk_i.ap().rearrange("i p n -> p i n")).then_inc(s_gl.h, 16)
        ins = gp.dma_start(BA(GWA, 4096).rearrange("p (i n) -> p i n", n=128),
                           gwat_i.ap().rearrange("i p n -> p i n"))
        s_gl.n += 16
        s_gl.inc(ins, 16)
    GP(_)
    gw_rdy = s_gl.n

    p7 = dict(qk_dv=0, h_dv=0, pt_ac=0, gb_te=0)
    for ts in range(8):
        def _(t, ts=ts, gr=gw_rdy, qv=p7["qk_dv"], hv=p7["h_dv"]):
            if ts == 0:
                w(t, s_gl, gr)
            w(t, s_dv, qv)
            for i in range(NI):
                ins = t.matmul(psA[:, 0:512], BA(XL2 + i * 1024 + ts * 128, 128),
                               BA(GWQ + i * 512, 512), start=(i == 0), stop=(i == NI - 1))
            s_te.inc(ins, 1)
            w(t, s_dv, hv)
            for i in range(NI):
                ins = t.matmul(psA[:, 512:640], BA(XL2 + i * 1024 + ts * 128, 128),
                               BA(GWA + i * 128, 128), start=(i == 0), stop=(i == NI - 1))
            s_te.inc(ins, 1)
        T(_)
        g_te = s_te.n

        def _(a, gt=g_te):
            w(a, s_te, gt)
            a.copy(qsb.ap(), psA[:, 0:512])
            ins = a.copy(hsb.ap(), psA[:, 512:640])
            s_ac.inc(ins, 1)
        A(_)
        qs_ac = s_ac.n

        def _(v, qa=qs_ac):
            w(v, s_ac, qa)
            v.tensor_tensor(s8b.ap(), qsb.ap()[:, 0:256], qsb.ap()[:, 256:512], ALU.mult)
            for e in range(8):
                ins = v.tensor_reduce(gsml.ap()[:, e:e + 1], s8b.ap()[:, 32 * e:32 * (e + 1)],
                                      AX.X, ALU.add)
            p7["qk_dv"] = s_dv.inc(ins, 1)
        V(_)
        mx_dv = s_dv.n

        def _(a, md=mx_dv):
            w(a, s_dv, md)
            ins = a.activation(gsml.ap()[:, 16:24], gsml.ap()[:, 0:8], AF.Exp,
                               scale=INVSQ_DK, accum_out=gsml.ap()[:, 10:11])
            s_ac.inc(ins, 1)
        A(_)
        ex_ac = s_ac.n

        def _(v, ea=ex_ac):
            w(v, s_ac, ea)
            ins = v.reciprocal(gsml.ap()[:, 11:12], gsml.ap()[:, 10:11])
            s_dv.inc(ins, 1)
        V(_)
        rc_dv = s_dv.n

        def _(a, rd=rc_dv):
            w(a, s_dv, rd)
            ins = a.mul(gsml.ap()[:, 12:13], gsml.ap()[:, 11:12], 1.0)
            s_ac.inc(ins, 1)
        A(_)
        rc_ac = s_ac.n

        def _(v, ra=rc_ac, gb=p7["gb_te"]):
            w(v, s_ac, ra)
            w(v, s_te, gb)
            for e in range(8):
                ins = v.tensor_scalar(gbf.ap()[:, 16 * e:16 * (e + 1)],
                                      hsb.ap()[:, 16 * e:16 * (e + 1)],
                                      gsml.ap()[:, 16 + e:17 + e],
                                      gsml.ap()[:, 12:13], ALU.mult, ALU.mult)
            p7["h_dv"] = s_dv.inc(ins, 1)
        V(_)
        g_dv = s_dv.n

        if debug and ts == 0:
            def _(sy, qa=qs_ac, gv=g_dv):
                w(sy, s_ac, qa)
                sy.dma_start(qkd.ap()[:, 0:512], qsb.ap()).then_inc(s_ss.h, 16)
                sy.dma_start(qkd.ap()[:, 512:640], hsb.ap()).then_inc(s_ss.h, 16)
                s_ss.n += 32
                w(sy, s_dv, gv)
                ins = sy.dma_start(gsd.ap(), gsml.ap())
                s_ss.inc(ins, 16)
            SY(_)

        def _(t, gd=g_dv, pa=p7["pt_ac"]):
            w(t, s_dv, gd)
            w(t, s_ac, pa)
            ins = t.transpose(psTb[:, 0:128], gbf.ap(), eyebf.ap())
            p7["gb_te"] = s_te.inc(ins, 1)
        T(_)
        tr_te = s_te.n

        def _(a, ts=ts, tt=tr_te):
            w(a, s_te, tt)
            ins = a.copy(gt_sb.ap()[:, ts * 128:(ts + 1) * 128], psTb[:, 0:128])
            p7["pt_ac"] = s_ac.inc(ins, 1)
        A(_)
    gt_done = s_ac.n

    if debug:
        def _(sy, gd=gt_done):
            w(sy, s_ac, gd)
            ins = sy.dma_start(gtd.ap(), gt_sb.ap())
            s_ss.inc(ins, 16)
        SY(_)
    fence()

    WFB = [WRK + 34816, WRK + 38912]
    BMT = WRK + 43008
    X2S8 = [0, 512]
    YB = [1024, 1536]

    def _(gp):
        ins = gp.dma_start(BA(BMT, 4096), bmt_i.ap())
        s_gl.inc(ins, 16)
    GP(_)
    bmt_rdy = s_gl.n

    p8 = dict(wf_te=[0, 0], o_dv=[0, 0], ob_ss=[0, 0])
    for o in range(32):
        op = o % 2

        def _(gp, o=o, op=op, wt_=p8["wf_te"][op]):
            w(gp, s_te, wt_)
            ins = gp.dma_start(BA(WFB[op], 4096), wf.ap()[o])
            s_gl.inc(ins, 16)
        GP(_)
        wf_rdy = s_gl.n

        for tc in range(2):
            def _(sy, o=o, tc=tc, xd=p8["o_dv"][tc]):
                w(sy, s_dv, xd)
                ins = sy.dma_start(FA(X2S8[tc], 512),
                                   x2d.ap()[:, o * 1024 + tc * 512: o * 1024 + (tc + 1) * 512])
                p8["x2_sl_%d" % tc] = s_sl.inc(ins, 16)
            SY(_)

            def _(t, o=o, op=op, tc=tc, wr=wf_rdy, br=bmt_rdy, ov=p8["o_dv"][tc]):
                pb = pbE if tc == 0 else pbF
                w(t, s_gl, wr)
                if o == 0:
                    w(t, s_gl, br)
                w(t, s_dv, ov)
                for i in range(NI):
                    t.matmul(pb[:, :], BA(WFB[op] + i * 128, 128),
                             BA(XL2 + i * 1024 + tc * 512, 512),
                             start=(i == 0), stop=False)
                ins = t.matmul(pb[:, :], BA(BMT + o * 128, 128),
                               gt_sb.ap()[:, tc * 512:(tc + 1) * 512],
                               start=False, stop=True)
                if tc == 1:
                    p8["wf_te"][op] = s_te.n + 1
                s_te.inc(ins, 1)
            T(_)
            f_te = s_te.n

            def _(v, o=o, tc=tc, ft=f_te, xr=p8["x2_sl_%d" % tc], os_=p8["ob_ss"][tc]):
                pb = pbE if tc == 0 else pbF
                w(v, s_te, ft)
                w(v, s_sl, xr)
                w(v, s_ss, os_)
                v.tensor_tensor(FA(YB[tc], 512), pb[:, :], FA(X2S8[tc], 512), ALU.add)
                ins = v.tensor_scalar(FA(YB[tc], 512), FA(YB[tc], 512),
                                      bcol_sb.ap()[:, o:o + 1], None, ALU.add)
                p8["o_dv"][tc] = s_dv.inc(ins, 1)
            V(_)
            o_dv = s_dv.n

            def _(sy, o=o, tc=tc, od=o_dv):
                w(sy, s_dv, od)
                ins = sy.dma_start(out.ap()[:, o * 1024 + tc * 512: o * 1024 + (tc + 1) * 512],
                                   FA(YB[tc], 512))
                p8["ob_ss"][tc] = s_ss.inc(ins, 16)
            SY(_)

    def _(sy):
        sy.wait_ge(s_ss.h, s_ss.n)
    SY(_)

    block.__exit__(None, None, None)
    return nc



def _pblk(r0, k):
    if k == 0:
        return r0
    if k == r0:
        return 0
    return k


def _phys_pos(r0):
    t = np.arange(S)
    g = t // 512
    blk = (t % 512) // 128
    off = t % 128
    tb = np.array([_pblk(r0, k) for k in range(4)])[blk]
    return g * 512 + tb * 128 + off


def _own_pos(r0):
    u = np.arange(1024)
    return 512 * (u // 128) + 128 * r0 + (u % 128)


def _rope_tables(pos, scale=1.0):
    inv = 1.0 / (10000.0 ** (np.arange(0, 64, dtype=np.float64) * 2.0 / 128.0))
    ang = pos.astype(np.float64)[None, :] * inv[:, None]
    cc = np.concatenate([np.cos(ang), np.cos(ang)], 0) * scale
    ss = np.concatenate([-np.sin(ang), np.sin(ang)], 0) * scale
    return (np.ascontiguousarray(cc.astype(np.float32)),
            np.ascontiguousarray(ss.astype(np.float32)))


def _tiles_T(wm):
    t = wm.reshape(32, 128, 32, 128)
    return np.ascontiguousarray(t.transpose(0, 3, 2, 1).reshape(32, 128, 4096))


def kernel(L0_attn_x, L0_attn_w_in, L0_attn_w_out, L0_ffn_W, L0_ffn_b,
           L0_ffn_Wq, L0_ffn_Wk, L0_ffn_A, L0_ffn_Bm,
           phases=8, debug=False):
    x = np.asarray(L0_attn_x, np.float32)
    w_in = np.asarray(L0_attn_w_in, np.float32)
    w_out = np.asarray(L0_attn_w_out, np.float32)
    W = np.asarray(L0_ffn_W, np.float32)
    b = np.asarray(L0_ffn_b, np.float32)
    Wq = np.asarray(L0_ffn_Wq, np.float32)
    Wk = np.asarray(L0_ffn_Wk, np.float32)
    Af = np.asarray(L0_ffn_A, np.float32).reshape(128, 4096)
    Bm = np.asarray(L0_ffn_Bm, np.float32)

    wq_img = _tiles_T(w_in[0:4096])
    wo_img = _tiles_T(w_out)
    wf_img = _tiles_T(W)
    wk_img = np.zeros((8, 128, 4096), np.float32)
    wv_img = np.zeros((8, 128, 4096), np.float32)
    for h in range(8):
        kw = w_in[4096 + 128 * h:4096 + 128 * (h + 1)]
        vw = w_in[5120 + 128 * h:5120 + 128 * (h + 1)]
        wk_img[h] = kw.reshape(128, 32, 128).transpose(2, 1, 0).reshape(128, 4096)
        wv_img[h] = vw.reshape(128, 32, 128).transpose(2, 1, 0).reshape(128, 4096)

    qk = np.concatenate([Wq, Wk], 0)
    gwqk_img = np.ascontiguousarray(qk.reshape(512, 32, 128).transpose(1, 2, 0))
    gwat_img = np.ascontiguousarray(Af.reshape(128, 32, 128).transpose(1, 2, 0))
    bmt_img = np.ascontiguousarray(Bm.transpose(0, 2, 1).reshape(128, 4096))
    bcol_img = np.ascontiguousarray(b.reshape(32, 128).T)
    eye_img = np.eye(128, dtype=np.float32)

    in_maps = []
    for c in range(8):
        bb, r0 = c // 4, c % 4
        pp = _phys_pos(r0)
        xT = x[bb].T
        xTp = xT[:, pp]
        xt_img = np.ascontiguousarray(
            xTp.reshape(32, 128, 8, 512).transpose(2, 1, 0, 3))

        cck, ssk = _rope_tables(pp)
        ccq, ssq = _rope_tables(_own_pos(r0), scale=1.0 / math.sqrt(128.0))

        kk = np.arange(128)[:, None]
        qq = np.arange(128)[None, :]
        m = np.zeros((128, 4, 4, 128), np.float32)
        for rr in range(4):
            vis = (128 * _pblk(r0, rr) + kk <= 128 * r0 + qq).astype(np.float32)
            for g4 in range(4):
                m[:, rr, g4, :] = vis
        msk_img = np.ascontiguousarray(m.reshape(128, 2048))

        in_maps.append(dict(
            xt=xt_img, wq=wq_img, wk=wk_img, wv=wv_img, wo=wo_img, wf=wf_img,
            gwqk_i=gwqk_img, gwat_i=gwat_img, bmt_i=bmt_img, bcol_i=bcol_img,
            cck_i=cck, ssk_i=ssk, ccq_i=ccq, ssq_i=ssq, msk_i=msk_img,
            eye_i=eye_img,
        ))

    nc = build_nc(phases=phases, debug=debug)
    res = run_bass_kernel_spmd(nc, in_maps, core_ids=list(range(8)))

    if debug:
        return res

    yfull = np.empty((B, S, E), np.float32)
    for c in range(8):
        bb, r0 = c // 4, c % 4
        r = np.asarray(res.results[c]["out"], np.float32)
        yc = r.reshape(128, 32, 1024).transpose(2, 1, 0).reshape(1024, 4096)
        yfull[bb, _own_pos(r0), :] = yc
    return yfull


# version 06
# speedup vs baseline: 1.0087x; 1.0087x over previous
import math
import numpy as np
import concourse.bass as bass
import concourse.mybir as mybir
from concourse.bass_utils import run_bass_kernel_spmd

F32 = mybir.dt.float32
BF16 = mybir.dt.bfloat16
AF = mybir.ActivationFunctionType
ALU = mybir.AluOpType
AX = mybir.AxisListType

B, S, E = 2, 4096, 4096
NI = 32
NH = 8
INVSQ_DK = float(1.0 / np.sqrt(np.float32(32.0)))
LN_EPS = 1e-5
EXP_SHIFT = -10.0

TB_CCK = 0
TB_SSK = 4096
TB_CCQ = 8192
TB_SSQ = 9216
TB_MSK = 10240
WRK = 12288
BA_COLS = 67584


def build_nc(phases=8, debug=False):
    nc = bass.Bass("TRN2", target_bir_lowering=False)
    dt = nc.dram_tensor

    xt = dt("xt", [8, 128, NI, 512], F32, kind="ExternalInput")
    wq = dt("wq", [32, 128, 4096], F32, kind="ExternalInput")
    wk = dt("wk", [8, 128, 4096], F32, kind="ExternalInput")
    wv = dt("wv", [8, 128, 4096], F32, kind="ExternalInput")
    wo = dt("wo", [32, 128, 4096], F32, kind="ExternalInput")
    wf = dt("wf", [32, 128, 4096], F32, kind="ExternalInput")
    gwqk_i = dt("gwqk_i", [32, 128, 512], F32, kind="ExternalInput")
    gwat_i = dt("gwat_i", [32, 128, 128], F32, kind="ExternalInput")
    bmt_i = dt("bmt_i", [128, 4096], F32, kind="ExternalInput")
    bcol_i = dt("bcol_i", [128, 32], F32, kind="ExternalInput")
    cck_i = dt("cck_i", [128, 4096], F32, kind="ExternalInput")
    ssk_i = dt("ssk_i", [128, 4096], F32, kind="ExternalInput")
    ccq_i = dt("ccq_i", [128, 1024], F32, kind="ExternalInput")
    ssq_i = dt("ssq_i", [128, 1024], F32, kind="ExternalInput")
    msk_i = dt("msk_i", [128, 2048], F32, kind="ExternalInput")
    eye_i = dt("eye_i", [128, 128], F32, kind="ExternalInput")
    out = dt("out", [128, 32768], F32, kind="ExternalOutput")

    dkind = "ExternalOutput" if debug else "Internal"
    oxln = dt("oxln", [128, 32768], BF16, kind=dkind)
    kd = dt("kd", [128, 32768], BF16, kind=dkind)
    vd = dt("vd", [128, 32768], BF16, kind=dkind)
    qd = dt("qd", [128, 32768], BF16, kind=dkind)
    aod = dt("aod", [128, 32768], BF16, kind=dkind)
    x2d = dt("x2d", [128, 32768], F32, kind=dkind)
    xl2d = dt("xl2d", [128, 32768], BF16, kind=dkind)
    gtd = dt("gtd", [128, 1024], BF16, kind=dkind)
    qkd = dt("qkd", [128, 640], F32, kind=dkind)
    gsd = dt("gsd", [128, 32], F32, kind=dkind)

    ba = nc.alloc_sbuf_tensor("ba", [128, BA_COLS], BF16)
    fa = nc.alloc_sbuf_tensor("fa", [128, 4608], F32)
    ones_c = nc.alloc_sbuf_tensor("ones_c", [128, 1], BF16)
    ones_cf = nc.alloc_sbuf_tensor("ones_cf", [128, 1], F32)
    ones_r = nc.alloc_sbuf_tensor("ones_r", [1, 128], BF16)
    epsc = nc.alloc_sbuf_tensor("epsc", [1, 1], F32)
    expb = nc.alloc_sbuf_tensor("expb", [128, 1], F32)
    bcol_sb = nc.alloc_sbuf_tensor("bcol_sb", [128, 32], F32)
    eyebf = nc.alloc_sbuf_tensor("eyebf", [128, 128], BF16)
    rows = nc.alloc_sbuf_tensor("rows", [1, 2048], F32)
    rows2 = nc.alloc_sbuf_tensor("rows2", [1, 2048], F32)
    rowsb = nc.alloc_sbuf_tensor("rowsb", [1, 2048], BF16)
    bcr = nc.alloc_sbuf_tensor("bcr", [128, 512], F32)
    bcmr = nc.alloc_sbuf_tensor("bcmr", [128, 512], F32)
    bcr2 = nc.alloc_sbuf_tensor("bcr2", [128, 1024], F32)
    bcmr2 = nc.alloc_sbuf_tensor("bcmr2", [128, 1024], F32)
    recb = nc.alloc_sbuf_tensor("recb", [128, 512], F32)
    gsml = nc.alloc_sbuf_tensor("gsml", [128, 32], F32)
    qsb = nc.alloc_sbuf_tensor("qsb", [128, 512], F32)
    hsb = nc.alloc_sbuf_tensor("hsb", [128, 128], F32)
    s8b = nc.alloc_sbuf_tensor("s8b", [128, 256], F32)
    gbf = nc.alloc_sbuf_tensor("gbf", [128, 128], BF16)
    gt_sb = nc.alloc_sbuf_tensor("gt_sb", [128, 1024], BF16)

    psA = nc.alloc_psum_tensor("psA", [128, 2048], F32).ap()
    pbE = nc.alloc_psum_tensor("pbE", [128, 512], F32).ap()
    pbF = nc.alloc_psum_tensor("pbF", [128, 512], F32).ap()
    pbD = nc.alloc_psum_tensor("pbD", [128, 512], F32).ap()
    psTb = nc.alloc_psum_tensor("psTb", [128, 1024], BF16).ap()

    def BA(a, n):
        return ba.ap()[:, a:a + n]

    def FA(a, n):
        return fa.ap()[:, a:a + n]

    class Sem:
        def __init__(self, name):
            self.h = nc.semaphore(name).__enter__()
            self.n = 0

        def inc(self, instr, k):
            instr.then_inc(self.h, k)
            self.n += k
            return self.n

    s_sl = Sem("s_sl")
    s_ss = Sem("s_ss")
    s_gl = Sem("s_gl")
    s_te = Sem("s_te")
    s_ac = Sem("s_ac")
    s_dv = Sem("s_dv")
    ALLS = [s_sl, s_ss, s_gl, s_te, s_ac, s_dv]

    block = nc.Block().__enter__()

    def T(fn):
        block.tensor(fn)

    def A(fn):
        block.scalar(fn)

    def V(fn):
        block.vector(fn)

    def GP(fn):
        block.gpsimd(fn)

    def SY(fn):
        block.sync(fn)

    def w(eng, sem, n):
        if n > 0:
            eng.wait_ge(sem.h, n)

    def fence():
        snap = [(s, s.n) for s in ALLS]

        def mk(eng):
            for s, n in snap:
                w(eng, s, n)
        T(mk)
        V(mk)
        A(mk)
        GP(mk)
        SY(mk)

    def _(g):
        g.dma_start(BA(TB_CCK, 4096), cck_i.ap()).then_inc(s_gl.h, 16)
        g.dma_start(BA(TB_SSK, 4096), ssk_i.ap()).then_inc(s_gl.h, 16)
        g.dma_start(BA(TB_CCQ, 1024), ccq_i.ap()).then_inc(s_gl.h, 16)
        g.dma_start(BA(TB_SSQ, 1024), ssq_i.ap()).then_inc(s_gl.h, 16)
        g.dma_start(BA(TB_MSK, 2048), msk_i.ap()).then_inc(s_gl.h, 16)
        g.dma_start(eyebf.ap(), eye_i.ap()).then_inc(s_gl.h, 16)
        s_gl.n += 96
    GP(_)

    def _(sy):
        sy.dma_start(bcol_sb.ap(), bcol_i.ap()).then_inc(s_sl.h, 16)
        s_sl.n += 16
    SY(_)

    def _(v):
        v.memset(ones_c.ap(), 1.0)
        v.memset(ones_cf.ap(), 1.0)
        v.memset(ones_r.ap(), 1.0)
        v.memset(epsc.ap(), LN_EPS)
        ins = v.memset(expb.ap(), EXP_SHIFT)
        s_dv.inc(ins, 1)
    V(_)
    fence()

    XG = [WRK, WRK + 16384]
    WKB = [WRK + 32768, WRK + 36864]
    WVB = [WRK + 40960, WRK + 45056]
    KEV = [WRK + 49152, WRK + 49664]
    VEVF = [WRK + 50176, WRK + 50688]
    VEVT = [WRK + 51200, WRK + 51712]
    SQB = [WRK + 52224, WRK + 52736]

    def xgv(par, i=None):
        if i is None:
            return BA(XG[par], 16384)
        return BA(XG[par] + i * 512, 512)

    st = dict(xg_te=[0, 0], xg_ss=[0, 0], kev_ss=[0, 0], vevf_dv=[0, 0],
              vevt_ss=[0, 0], kv_dv=0, vf_te=[0, 0], pstb_dv=[0, 0],
              wk_te=[0, 0], bc_dv=0, fin_dv=0, last_norm_dv=0)

    for pair in range(4):
        glist = [2 * pair, 2 * pair + 1]
        norm_dv_g = [0, 0]
        for par, g in enumerate(glist):
            def _(gp, g=g, par=par):
                w(gp, s_te, st["xg_te"][par])
                w(gp, s_ss, st["xg_ss"][par])
                flat = xt.ap()[g].rearrange("p i t -> p (i t)")
                for q4 in range(4):
                    ins = gp.dma_start(BA(XG[par] + q4 * 4096, 4096),
                                       flat[:, q4 * 4096:(q4 + 1) * 4096])
                    s_gl.inc(ins, 16)
            GP(_)
            xg_rdy = s_gl.n

            te0 = s_te.n
            ac0 = s_ac.n
            for i in range(NI):
                def _(a, i=i, par=par, xr=xg_rdy, te0=te0):
                    if i % 8 == 0:
                        w(a, s_gl, xr - 48 + 16 * (i // 8))
                    if i >= 2:
                        w(a, s_te, te0 + i - 1)
                    ins = a.activation(BA(SQB[i % 2], 512), xgv(par, i), AF.Square)
                    s_ac.inc(ins, 1)
                A(_)

                def _(t, i=i, par=par, xr=xg_rdy, ac0=ac0, fp=st["fin_dv"]):
                    if i % 8 == 0:
                        w(t, s_gl, xr - 48 + 16 * (i // 8))
                    if i == 0:
                        w(t, s_dv, fp)
                    t.matmul(psA[0:1, 0:512], ones_c.ap(), xgv(par, i),
                             start=(i == 0), stop=(i == NI - 1))
                    w(t, s_ac, ac0 + i + 1)
                    ins = t.matmul(psA[0:1, 512:1024], ones_c.ap(), BA(SQB[i % 2], 512),
                                   start=(i == 0), stop=(i == NI - 1))
                    s_te.inc(ins, 1)
                T(_)
            stats_te = s_te.n

            def _(v, sd=stats_te):
                w(v, s_te, sd)
                v.tensor_scalar(rows.ap()[:, 0:512], psA[0:1, 0:512],
                                1.0 / 4096.0, None, ALU.mult)
                v.tensor_scalar(rows.ap()[:, 1536:2048], psA[0:1, 512:1024],
                                1.0 / 4096.0, None, ALU.mult)
                v.tensor_tensor(rows.ap()[:, 512:1024], rows.ap()[:, 0:512],
                                rows.ap()[:, 0:512], ALU.mult)
                ins = v.tensor_tensor(rows.ap()[:, 1536:2048], rows.ap()[:, 1536:2048],
                                      rows.ap()[:, 512:1024], ALU.subtract)
                s_dv.inc(ins, 1)
            V(_)
            st["fin_dv"] = s_dv.n

            def _(a, vd_=s_dv.n):
                w(a, s_dv, vd_)
                ins = a.activation(rows.ap()[:, 512:1024], rows.ap()[:, 1536:2048],
                                   AF.Sqrt, bias=epsc.ap())
                s_ac.inc(ins, 1)
            A(_)
            sd_ac = s_ac.n

            def _(v, sd=sd_ac):
                w(v, s_ac, sd)
                v.reciprocal(rows.ap()[:, 512:1024], rows.ap()[:, 512:1024])
                v.tensor_tensor(rows.ap()[:, 1024:1536], rows.ap()[:, 0:512],
                                rows.ap()[:, 512:1024], ALU.mult)
                v.tensor_copy(rowsb.ap()[:, 0:512], rows.ap()[:, 512:1024])
                ins = v.tensor_copy(rowsb.ap()[:, 512:1024], rows.ap()[:, 1024:1536])
                s_dv.inc(ins, 1)
            V(_)
            rows_dv = s_dv.n

            def _(t, rd=rows_dv, ba_=st["bc_dv"]):
                w(t, s_dv, rd)
                w(t, s_dv, ba_)
                t.matmul(psA[:, 1024:1536], ones_r.ap(), rowsb.ap()[:, 0:512],
                         start=True, stop=True)
                ins = t.matmul(psA[:, 1536:2048], ones_r.ap(), rowsb.ap()[:, 512:1024],
                               start=True, stop=True)
                s_te.inc(ins, 1)
            T(_)
            bc_te = s_te.n

            def _(v, par=par, bt=bc_te):
                w(v, s_te, bt)
                v.tensor_copy(bcr.ap(), psA[:, 1024:1536])
                ins = v.tensor_copy(bcmr.ap(), psA[:, 1536:2048])
                st["bc_dv"] = s_dv.inc(ins, 1)
                for i in range(NI):
                    v.tensor_tensor(xgv(par, i), xgv(par, i), bcr.ap(), ALU.mult)
                    ins = v.tensor_tensor(xgv(par, i), xgv(par, i), bcmr.ap(), ALU.subtract)
                s_dv.inc(ins, 1)
            V(_)
            norm_dv_g[par] = s_dv.n
            st["last_norm_dv"] = s_dv.n

            def _(sy, g=g, par=par, nd=s_dv.n):
                w(sy, s_dv, nd)
                dst = oxln.ap().rearrange("p (i g q) -> p i g q", g=8, q=128)[:, :, g, :]
                src = BA(XG[par], 16384).rearrange("p (i t) -> p i t", t=512)[:, :, 0:128]
                ins = sy.dma_start(dst, src)
                st["xg_ss"][par] = s_ss.inc(ins, 16)
            SY(_)

        for h in range(NH):
            hp = h % 2

            def _(gp, h=h, hp=hp):
                w(gp, s_te, st["wk_te"][hp])
                gp.dma_start(BA(WKB[hp], 4096), wk.ap()[h]).then_inc(s_gl.h, 16)
                ins = gp.dma_start(BA(WVB[hp], 4096), wv.ap()[h])
                s_gl.n += 16
                s_gl.inc(ins, 16)
            GP(_)
            wkv_rdy = s_gl.n

            for par, g in enumerate(glist):
                def _(t, hp=hp, par=par, wr=wkv_rdy, nd=norm_dv_g[par], kv=st["kv_dv"]):
                    w(t, s_gl, wr)
                    w(t, s_dv, nd)
                    w(t, s_dv, kv)
                    for i in range(NI):
                        ins = t.matmul(pbE[:, :], BA(WKB[hp] + i * 128, 128), xgv(par, i),
                                       start=(i == 0), stop=(i == NI - 1))
                    s_te.inc(ins, 1)
                T(_)
                k_te = s_te.n

                def _(v, g=g, par=par, kt=k_te, ks=st["kev_ss"][par]):
                    w(v, s_te, kt)
                    w(v, s_ss, ks)
                    v.tensor_copy(FA(0, 512)[0:64, :], pbE[64:128, :])
                    v.tensor_copy(FA(0, 512)[64:128, :], pbE[0:64, :])
                    v.tensor_tensor(FA(512, 512), pbE[:, :], BA(TB_CCK + g * 512, 512), ALU.mult)
                    v.tensor_tensor(FA(0, 512), FA(0, 512), BA(TB_SSK + g * 512, 512), ALU.mult)
                    ins = v.tensor_tensor(BA(KEV[par], 512), FA(512, 512), FA(0, 512), ALU.add)
                    st["kv_dv"] = s_dv.inc(ins, 1)
                V(_)

                def _(sy, h=h, g=g, par=par, kdv=st["kv_dv"]):
                    w(sy, s_dv, kdv)
                    ins = sy.dma_start(kd.ap()[:, h * 4096 + g * 512: h * 4096 + (g + 1) * 512],
                                       BA(KEV[par], 512))
                    st["kev_ss"][par] = s_ss.inc(ins, 16)
                SY(_)

                def _(t, hp=hp, par=par, va=st["vevf_dv"][par]):
                    w(t, s_dv, va)
                    for i in range(NI):
                        ins = t.matmul(pbF[:, :], BA(WVB[hp] + i * 128, 128), xgv(par, i),
                                       start=(i == 0), stop=(i == NI - 1))
                    s_te.inc(ins, 1)
                T(_)
                v_te = s_te.n

                def _(v, vt=v_te, par=par, ft=st["vf_te"][par]):
                    w(v, s_te, vt)
                    w(v, s_te, ft)
                    ins = v.tensor_copy(BA(VEVF[par], 512), pbF[:, :])
                    st["vevf_dv"][par] = s_dv.inc(ins, 1)
                V(_)
                vf_dv = s_dv.n

                def _(t, par=par, va=vf_dv, pa=st["pstb_dv"][par]):
                    w(t, s_dv, va)
                    w(t, s_dv, pa)
                    for ts in range(4):
                        ins = t.transpose(psTb[:, par * 512 + ts * 128: par * 512 + (ts + 1) * 128],
                                          BA(VEVF[par] + ts * 128, 128), eyebf.ap())
                    st["vf_te"][par] = s_te.inc(ins, 1)
                T(_)
                tr_te = s_te.n

                def _(v, par=par, tt=tr_te, vs=st["vevt_ss"][par]):
                    w(v, s_te, tt)
                    w(v, s_ss, vs)
                    ins = v.tensor_copy(BA(VEVT[par], 512), psTb[:, par * 512:(par + 1) * 512])
                    st["pstb_dv"][par] = s_dv.inc(ins, 1)
                V(_)
                vt_dv = s_dv.n

                def _(sy, h=h, g=g, par=par, va=vt_dv):
                    w(sy, s_dv, va)
                    ins = sy.dma_start(vd.ap()[:, h * 4096 + g * 512: h * 4096 + (g + 1) * 512],
                                       BA(VEVT[par], 512))
                    st["vevt_ss"][par] = s_ss.inc(ins, 16)
                SY(_)
            st["wk_te"][hp] = s_te.n
        st["xg_te"][0] = s_te.n
        st["xg_te"][1] = s_te.n

    fence()
    if phases <= 1:
        def _(sy):
            sy.wait_ge(s_ss.h, s_ss.n)
        SY(_)
        block.__exit__(None, None, None)
        return nc

    OXB = WRK
    WQB = [WRK + 32768, WRK + 36864]
    QEV = [WRK + 40960, WRK + 41472, WRK + 41984, WRK + 42496]

    def _(sy):
        ins = sy.dma_start(BA(OXB, 32768), oxln.ap())
        s_sl.inc(ins, 16)
    SY(_)
    ox_rdy = s_sl.n

    p2 = dict(wq_te=[0, 0], q_dv=[0, 0], qev_ss=[0, 0, 0, 0])
    for f in range(32):
        fp = f % 2
        h, g4 = f // 4, f % 4

        def _(gp, f=f, fp=fp):
            w(gp, s_te, p2["wq_te"][fp])
            ins = gp.dma_start(BA(WQB[fp], 4096), wq.ap()[f])
            s_gl.inc(ins, 16)
        GP(_)
        wq_rdy = s_gl.n

        for half in range(2):
            def _(t, f=f, fp=fp, half=half, wr=wq_rdy, orr=ox_rdy, qv=p2["q_dv"][half]):
                pb = pbE if half == 0 else pbF
                if f == 0 and half == 0:
                    w(t, s_sl, orr)
                w(t, s_gl, wr)
                w(t, s_dv, qv)
                for i in range(NI):
                    ins = t.matmul(pb[:, :], BA(WQB[fp] + i * 128, 128),
                                   BA(OXB + i * 1024 + half * 512, 512),
                                   start=(i == 0), stop=(i == NI - 1))
                if half == 1:
                    p2["wq_te"][fp] = s_te.n + 1
                s_te.inc(ins, 1)
            T(_)
            q_te = s_te.n

            qe = 2 * fp + half

            def _(v, half=half, qt=q_te, qe=qe, qs=p2["qev_ss"][qe]):
                pb = pbE if half == 0 else pbF
                w(v, s_te, qt)
                w(v, s_ss, qs)
                v.tensor_copy(FA(0, 512)[0:64, :], pb[64:128, :])
                v.tensor_copy(FA(0, 512)[64:128, :], pb[0:64, :])
                v.tensor_tensor(FA(512, 512), pb[:, :], BA(TB_CCQ + half * 512, 512), ALU.mult)
                v.tensor_tensor(FA(0, 512), FA(0, 512), BA(TB_SSQ + half * 512, 512), ALU.mult)
                ins = v.tensor_tensor(BA(QEV[qe], 512), FA(512, 512), FA(0, 512), ALU.add)
                p2["q_dv"][half] = s_dv.inc(ins, 1)
            V(_)

            def _(sy, h=h, g4=g4, half=half, qe=qe, qdv=p2["q_dv"][half]):
                w(sy, s_dv, qdv)
                dst = qd.ap().rearrange("p (hh j g q) -> p hh j g q", hh=8, j=8, g=4)[
                    :, h, 4 * half:4 * half + 4, g4, :]
                src = BA(QEV[qe], 512).rearrange("p (j q) -> p j q", j=4)
                ins = sy.dma_start(dst, src)
                p2["qev_ss"][qe] = s_ss.inc(ins, 16)
            SY(_)

    fence()
    if phases <= 2:
        def _(sy):
            sy.wait_ge(s_ss.h, s_ss.n)
        SY(_)
        block.__exit__(None, None, None)
        return nc

    KHB = [WRK, WRK + 4096]
    VBF = [WRK + 8192, WRK + 12288]
    QTB = [WRK + 16384, WRK + 16896]
    PRG = [WRK + 17408, WRK + 19456]
    AO = WRK + 21504

    a3 = dict(kv_te=[0, 0], qt_te=[0, 0], prg_te=[0, 0],
              pbO_dv=[0, 0], den_dv=0, rb_dv=0,
              psA_ac_0=0, psA_ac_1=0)

    for h in range(NH):
        par = h % 2

        def _(sy, h=h, par=par, kt=a3["kv_te"][par]):
            w(sy, s_te, kt)
            sy.dma_start(BA(KHB[par], 4096),
                         kd.ap()[:, h * 4096:(h + 1) * 4096]).then_inc(s_sl.h, 16)
            ins = sy.dma_start(BA(VBF[par], 4096),
                               vd.ap()[:, h * 4096:(h + 1) * 4096])
            s_sl.n += 16
            s_sl.inc(ins, 16)
        SY(_)
        kv_rdy = s_sl.n

        for j in range(NH):
            jp = j % 2

            def _(sy, h=h, j=j, jp=jp, qt=a3["qt_te"][jp]):
                w(sy, s_te, qt)
                ins = sy.dma_start(BA(QTB[jp], 512),
                                   qd.ap()[:, h * 4096 + j * 512: h * 4096 + (j + 1) * 512])
                s_sl.inc(ins, 16)
            SY(_)
            qt_rdy = s_sl.n

            for c2 in range(j + 1):
                pp = c2 % 2

                for bk in range(2):
                    def _(t, j=j, jp=jp, c2=c2, par=par, bk=bk, qr=qt_rdy, kr=kv_rdy,
                          pa=a3["psA_ac_%d" % bk], po=a3["pbO_dv"][jp],
                          rd=a3.get("rb_dv", 0)):
                        if c2 == 0 and bk == 0:
                            w(t, s_sl, qr)
                            w(t, s_sl, kr)
                            w(t, s_dv, po)
                            if bk == 0:
                                w(t, s_dv, rd)
                        w(t, s_ac, pa)
                        for rr in (0, 1):
                            kt = c2 * 4 + 2 * bk + rr
                            ins = t.matmul(psA[:, (2 * bk + rr) * 512:(2 * bk + rr + 1) * 512],
                                           BA(KHB[par] + kt * 128, 128), BA(QTB[jp], 512),
                                           start=True, stop=True)
                        s_te.inc(ins, 1)
                    T(_)
                    sc_te = s_te.n

                    def _(a, pp=pp, bk=bk, st_=sc_te, pt=a3["prg_te"][pp]):
                        w(a, s_te, st_)
                        if bk == 0:
                            w(a, s_te, pt)
                        ins = a.activation(BA(PRG[pp] + bk * 1024, 1024),
                                           psA[:, bk * 1024:(bk + 1) * 1024],
                                           AF.Exp, bias=expb.ap())
                        a3["psA_ac_%d" % bk] = s_ac.inc(ins, 1)
                    A(_)
                exp_ac = s_ac.n

                mask_dv = 0
                if c2 == j:
                    def _(v, pp=pp, ea=exp_ac):
                        w(v, s_ac, ea)
                        ins = v.tensor_tensor(BA(PRG[pp], 2048), BA(PRG[pp], 2048),
                                              BA(TB_MSK, 2048), ALU.mult)
                        s_dv.inc(ins, 1)
                    V(_)
                    mask_dv = s_dv.n

                def _(t, j=j, jp=jp, c2=c2, par=par, pp=pp, ea=exp_ac, md=mask_dv,
                      dd=a3["den_dv"]):
                    pbO = pbE if jp == 0 else pbF
                    if md:
                        w(t, s_dv, md)
                    else:
                        w(t, s_ac, ea)
                    if c2 == 0:
                        w(t, s_dv, dd)
                    for rr in range(4):
                        kt = c2 * 4 + rr
                        t.matmul(pbD[0:1, :], ones_c.ap(), BA(PRG[pp] + rr * 512, 512),
                                 start=(c2 == 0 and rr == 0), stop=(c2 == j and rr == 3))
                        ins = t.matmul(pbO[:, :], BA(VBF[par] + kt * 128, 128),
                                       BA(PRG[pp] + rr * 512, 512),
                                       start=(c2 == 0 and rr == 0), stop=(c2 == j and rr == 3))
                    a3["prg_te"][pp] = s_te.inc(ins, 1)
                T(_)
            den_te = s_te.n
            a3["qt_te"][jp] = s_te.n

            def _(v, dt_=den_te, rt=a3.get("rb_te", 0)):
                w(v, s_te, dt_)
                w(v, s_te, rt)
                v.reciprocal(rows.ap()[:, 0:512], pbD[0:1, :])
                ins = v.tensor_copy(rowsb.ap()[:, 0:512], rows.ap()[:, 0:512])
                a3["den_dv"] = s_dv.inc(ins, 1)
            V(_)
            den_dv = s_dv.n

            def _(t, dd=den_dv, pa=a3["psA_ac_0"]):
                w(t, s_dv, dd)
                w(t, s_ac, pa)
                ins = t.matmul(psA[:, 0:512], ones_r.ap(), rowsb.ap()[:, 0:512],
                               start=True, stop=True)
                a3["rb_te"] = s_te.inc(ins, 1)
            T(_)
            rb_te = s_te.n

            def _(v, h=h, j=j, jp=jp, rt=rb_te):
                pbO = pbE if jp == 0 else pbF
                w(v, s_te, rt)
                v.tensor_copy(recb.ap(), psA[:, 0:512])
                ins = v.engine_nop()
                a3["rb_dv"] = s_dv.inc(ins, 1)
                dst = BA(AO, 32768).rearrange("p (e u) -> p e u", u=1024)[
                    :, 4 * h:4 * h + 4, j * 128:(j + 1) * 128]
                ins = v.tensor_tensor(dst, pbO.rearrange("p (g q) -> p g q", g=4),
                                      recb.ap().rearrange("p (g q) -> p g q", g=4), ALU.mult)
                a3["pbO_dv"][jp] = s_dv.inc(ins, 1)
            V(_)
        a3["kv_te"][par] = s_te.n
    ao_dv = s_dv.n

    if debug:
        def _(sy, ad=ao_dv):
            w(sy, s_dv, ad)
            ins = sy.dma_start(aod.ap(), BA(AO, 32768))
            s_ss.inc(ins, 16)
        SY(_)

    fence()
    if phases <= 3:
        def _(sy):
            sy.wait_ge(s_ss.h, s_ss.n)
        SY(_)
        block.__exit__(None, None, None)
        return nc

    WOB = [WRK, WRK + 4096]
    XRS = [0, 512]
    X2B = [1024, 1536]
    SQ2 = [2048, 2560]

    p5 = dict(wo_te=[0, 0], x2_dv=[0, 0], st_te=[0, 0], x2_ss=[0, 0])
    for o in range(32):
        op = o % 2

        def _(gp, o=o, op=op, wt_=p5["wo_te"][op]):
            w(gp, s_te, wt_)
            ins = gp.dma_start(BA(WOB[op], 4096), wo.ap()[o])
            s_gl.inc(ins, 16)
        GP(_)
        wo_rdy = s_gl.n

        for tc in range(2):
            def _(sy, o=o, tc=tc, xd=p5["x2_dv"][tc]):
                w(sy, s_dv, xd)
                src = xt.ap().rearrange("g p i t -> p g i t")[:, 4 * tc:4 * tc + 4, o, 0:128]
                ins = sy.dma_start(FA(XRS[tc], 512).rearrange("p (g q) -> p g q", g=4), src)
                p5["xr_sl_%d" % tc] = s_sl.inc(ins, 16)
            SY(_)

            def _(t, o=o, op=op, tc=tc, wr=wo_rdy, xv=p5["x2_dv"][tc]):
                pb = pbE if tc == 0 else pbF
                w(t, s_gl, wr)
                w(t, s_dv, xv)
                for e in range(NI):
                    ins = t.matmul(pb[:, :], BA(WOB[op] + e * 128, 128),
                                   BA(AO + e * 1024 + tc * 512, 512),
                                   start=(e == 0), stop=(e == NI - 1))
                if tc == 1:
                    p5["wo_te"][op] = s_te.n + 1
                s_te.inc(ins, 1)
            T(_)
            y_te = s_te.n

            def _(v, tc=tc, yt=y_te, xr=p5["xr_sl_%d" % tc], xs=p5["x2_ss"][tc],
                  sq=p5["st_te"][tc]):
                pb = pbE if tc == 0 else pbF
                w(v, s_te, yt)
                w(v, s_sl, xr)
                w(v, s_ss, xs)
                w(v, s_te, sq)
                v.tensor_tensor(FA(X2B[tc], 512), pb[:, :], FA(XRS[tc], 512), ALU.add)
                ins = v.tensor_tensor(FA(SQ2[tc], 512), FA(X2B[tc], 512),
                                      FA(X2B[tc], 512), ALU.mult)
                p5["x2_dv"][tc] = s_dv.inc(ins, 1)
            V(_)
            x2_dv = s_dv.n

            def _(sy, o=o, tc=tc, xd=x2_dv):
                w(sy, s_dv, xd)
                ins = sy.dma_start(x2d.ap()[:, o * 1024 + tc * 512: o * 1024 + (tc + 1) * 512],
                                   FA(X2B[tc], 512))
                p5["x2_ss"][tc] = s_ss.inc(ins, 16)
            SY(_)

            def _(t, o=o, tc=tc, xd=x2_dv):
                w(t, s_dv, xd)
                t.matmul(psA[0:1, tc * 512:(tc + 1) * 512], ones_cf.ap(), FA(X2B[tc], 512),
                         start=(o == 0), stop=(o == 31))
                ins = t.matmul(psA[0:1, 1024 + tc * 512:1024 + (tc + 1) * 512], ones_cf.ap(),
                               FA(SQ2[tc], 512), start=(o == 0), stop=(o == 31))
                p5["st_te"][tc] = s_te.inc(ins, 1)
            T(_)
    st5_te = s_te.n

    def _(v, sd=st5_te):
        w(v, s_te, sd)
        v.tensor_scalar(rows2.ap()[:, 0:1024], psA[0:1, 0:1024], 1.0 / 4096.0, None, ALU.mult)
        v.tensor_scalar(rows.ap()[:, 1024:2048], psA[0:1, 1024:2048], 1.0 / 4096.0, None, ALU.mult)
        v.tensor_tensor(rows2.ap()[:, 1024:2048], rows2.ap()[:, 0:1024],
                        rows2.ap()[:, 0:1024], ALU.mult)
        ins = v.tensor_tensor(rows.ap()[:, 1024:2048], rows.ap()[:, 1024:2048],
                              rows2.ap()[:, 1024:2048], ALU.subtract)
        s_dv.inc(ins, 1)
    V(_)
    v2_dv = s_dv.n

    def _(a, vd_=v2_dv):
        w(a, s_dv, vd_)
        ins = a.activation(rows2.ap()[:, 1024:2048], rows.ap()[:, 1024:2048],
                           AF.Sqrt, bias=epsc.ap())
        s_ac.inc(ins, 1)
    A(_)
    sd2_ac = s_ac.n

    def _(v, sd=sd2_ac):
        w(v, s_ac, sd)
        v.reciprocal(rows2.ap()[:, 1024:2048], rows2.ap()[:, 1024:2048])
        v.tensor_tensor(rows.ap()[:, 0:1024], rows2.ap()[:, 0:1024],
                        rows2.ap()[:, 1024:2048], ALU.mult)
        v.tensor_copy(rowsb.ap()[:, 0:1024], rows2.ap()[:, 1024:2048])
        ins = v.tensor_copy(rowsb.ap()[:, 1024:2048], rows.ap()[:, 0:1024])
        s_dv.inc(ins, 1)
    V(_)
    r2_dv = s_dv.n

    def _(t, rd=r2_dv):
        w(t, s_dv, rd)
        t.matmul(psA[:, 0:512], ones_r.ap(), rowsb.ap()[:, 0:512], start=True, stop=True)
        t.matmul(psA[:, 512:1024], ones_r.ap(), rowsb.ap()[:, 512:1024], start=True, stop=True)
        t.matmul(psA[:, 1024:1536], ones_r.ap(), rowsb.ap()[:, 1024:1536], start=True, stop=True)
        ins = t.matmul(psA[:, 1536:2048], ones_r.ap(), rowsb.ap()[:, 1536:2048],
                       start=True, stop=True)
        s_te.inc(ins, 1)
    T(_)
    bc2_te = s_te.n

    def _(v, bt=bc2_te):
        w(v, s_te, bt)
        v.tensor_copy(bcr2.ap(), psA[:, 0:1024])
        ins = v.tensor_copy(bcmr2.ap(), psA[:, 1024:2048])
        s_dv.inc(ins, 1)
    V(_)
    fence()
    if phases <= 5:
        def _(sy):
            sy.wait_ge(s_ss.h, s_ss.n)
        SY(_)
        block.__exit__(None, None, None)
        return nc

    XL2 = WRK + 2048
    X2S = [3072, 3584]

    p6 = dict(xl_dv=[0, 0])
    for i in range(NI):
        for tc in range(2):
            pp = (2 * i + tc) % 2

            def _(sy, i=i, tc=tc, pp=pp, xd=p6["xl_dv"][pp]):
                w(sy, s_dv, xd)
                ins = sy.dma_start(FA(X2S[pp], 512),
                                   x2d.ap()[:, i * 1024 + tc * 512: i * 1024 + (tc + 1) * 512])
                s_sl.inc(ins, 16)
            SY(_)
            x2_rdy = s_sl.n

            def _(v, i=i, tc=tc, pp=pp, xr=x2_rdy):
                w(v, s_sl, xr)
                v.tensor_tensor(FA(X2S[pp], 512), FA(X2S[pp], 512),
                                bcr2.ap()[:, tc * 512:(tc + 1) * 512], ALU.mult)
                ins = v.tensor_tensor(BA(XL2 + i * 1024 + tc * 512, 512), FA(X2S[pp], 512),
                                      bcmr2.ap()[:, tc * 512:(tc + 1) * 512], ALU.subtract)
                p6["xl_dv"][pp] = s_dv.inc(ins, 1)
            V(_)
    xl2_dv = s_dv.n

    if debug:
        def _(sy, xd=xl2_dv):
            w(sy, s_dv, xd)
            ins = sy.dma_start(xl2d.ap(), BA(XL2, 32768))
            s_ss.inc(ins, 16)
        SY(_)
    fence()

    GWQ = WRK + 34816
    GWA = WRK + 51200

    def _(gp):
        gp.dma_start(BA(GWQ, 16384).rearrange("p (i n) -> p i n", n=512),
                     gwqk_i.ap().rearrange("i p n -> p i n")).then_inc(s_gl.h, 16)
        ins = gp.dma_start(BA(GWA, 4096).rearrange("p (i n) -> p i n", n=128),
                           gwat_i.ap().rearrange("i p n -> p i n"))
        s_gl.n += 16
        s_gl.inc(ins, 16)
    GP(_)
    gw_rdy = s_gl.n

    p7 = dict(qk_dv=0, h_dv=0, pt_dv=0, gb_te=0)
    for ts in range(8):
        def _(t, ts=ts, gr=gw_rdy, qv=p7["qk_dv"], hv=p7["h_dv"]):
            if ts == 0:
                w(t, s_gl, gr)
            w(t, s_dv, qv)
            for i in range(NI):
                ins = t.matmul(psA[:, 0:512], BA(XL2 + i * 1024 + ts * 128, 128),
                               BA(GWQ + i * 512, 512), start=(i == 0), stop=(i == NI - 1))
            s_te.inc(ins, 1)
            w(t, s_dv, hv)
            for i in range(NI):
                ins = t.matmul(psA[:, 512:640], BA(XL2 + i * 1024 + ts * 128, 128),
                               BA(GWA + i * 128, 128), start=(i == 0), stop=(i == NI - 1))
            s_te.inc(ins, 1)
        T(_)
        g_te = s_te.n

        def _(v, gt=g_te):
            w(v, s_te, gt)
            v.tensor_copy(qsb.ap(), psA[:, 0:512])
            v.tensor_copy(hsb.ap(), psA[:, 512:640])
            v.tensor_tensor(s8b.ap(), qsb.ap()[:, 0:256], qsb.ap()[:, 256:512], ALU.mult)
            for e in range(8):
                ins = v.tensor_reduce(gsml.ap()[:, e:e + 1], s8b.ap()[:, 32 * e:32 * (e + 1)],
                                      AX.X, ALU.add)
            p7["qk_dv"] = s_dv.inc(ins, 1)
        V(_)
        mx_dv = s_dv.n

        def _(a, md=mx_dv):
            w(a, s_dv, md)
            ins = a.activation(gsml.ap()[:, 16:24], gsml.ap()[:, 0:8], AF.Exp,
                               scale=INVSQ_DK, accum_out=gsml.ap()[:, 10:11])
            s_ac.inc(ins, 1)
        A(_)
        ex_ac = s_ac.n

        def _(v, ea=ex_ac):
            w(v, s_ac, ea)
            ins = v.reciprocal(gsml.ap()[:, 11:12], gsml.ap()[:, 10:11])
            s_dv.inc(ins, 1)
        V(_)
        rc_dv = s_dv.n

        def _(a, rd=rc_dv):
            w(a, s_dv, rd)
            ins = a.mul(gsml.ap()[:, 12:13], gsml.ap()[:, 11:12], 1.0)
            s_ac.inc(ins, 1)
        A(_)
        rc_ac = s_ac.n

        def _(v, ra=rc_ac, gb=p7["gb_te"]):
            w(v, s_ac, ra)
            w(v, s_te, gb)
            for e in range(8):
                ins = v.tensor_scalar(gbf.ap()[:, 16 * e:16 * (e + 1)],
                                      hsb.ap()[:, 16 * e:16 * (e + 1)],
                                      gsml.ap()[:, 16 + e:17 + e],
                                      gsml.ap()[:, 12:13], ALU.mult, ALU.mult)
            p7["h_dv"] = s_dv.inc(ins, 1)
        V(_)
        g_dv = s_dv.n

        if debug and ts == 0:
            def _(sy, qa=qs_ac, gv=g_dv):
                w(sy, s_ac, qa)
                sy.dma_start(qkd.ap()[:, 0:512], qsb.ap()).then_inc(s_ss.h, 16)
                sy.dma_start(qkd.ap()[:, 512:640], hsb.ap()).then_inc(s_ss.h, 16)
                s_ss.n += 32
                w(sy, s_dv, gv)
                ins = sy.dma_start(gsd.ap(), gsml.ap())
                s_ss.inc(ins, 16)
            SY(_)

        def _(t, gd=g_dv, pa=p7["pt_dv"]):
            w(t, s_dv, gd)
            w(t, s_dv, pa)
            ins = t.transpose(psTb[:, 0:128], gbf.ap(), eyebf.ap())
            p7["gb_te"] = s_te.inc(ins, 1)
        T(_)
        tr_te = s_te.n

        def _(v, ts=ts, tt=tr_te):
            w(v, s_te, tt)
            ins = v.tensor_copy(gt_sb.ap()[:, ts * 128:(ts + 1) * 128], psTb[:, 0:128])
            p7["pt_dv"] = s_dv.inc(ins, 1)
        V(_)
    gt_done = s_dv.n

    if debug:
        def _(sy, gd=gt_done):
            w(sy, s_dv, gd)
            ins = sy.dma_start(gtd.ap(), gt_sb.ap())
            s_ss.inc(ins, 16)
        SY(_)
    fence()

    WFB = [WRK + 34816, WRK + 38912]
    BMT = WRK + 43008
    X2S8 = [0, 512]
    YB = [1024, 1536]

    def _(gp):
        ins = gp.dma_start(BA(BMT, 4096), bmt_i.ap())
        s_gl.inc(ins, 16)
    GP(_)
    bmt_rdy = s_gl.n

    p8 = dict(wf_te=[0, 0], o_dv=[0, 0], ob_ss=[0, 0])
    for o in range(32):
        op = o % 2

        def _(gp, o=o, op=op, wt_=p8["wf_te"][op]):
            w(gp, s_te, wt_)
            ins = gp.dma_start(BA(WFB[op], 4096), wf.ap()[o])
            s_gl.inc(ins, 16)
        GP(_)
        wf_rdy = s_gl.n

        for tc in range(2):
            def _(sy, o=o, tc=tc, xd=p8["o_dv"][tc]):
                w(sy, s_dv, xd)
                ins = sy.dma_start(FA(X2S8[tc], 512),
                                   x2d.ap()[:, o * 1024 + tc * 512: o * 1024 + (tc + 1) * 512])
                p8["x2_sl_%d" % tc] = s_sl.inc(ins, 16)
            SY(_)

            def _(t, o=o, op=op, tc=tc, wr=wf_rdy, br=bmt_rdy, ov=p8["o_dv"][tc]):
                pb = pbE if tc == 0 else pbF
                w(t, s_gl, wr)
                if o == 0:
                    w(t, s_gl, br)
                    w(t, s_dv, gt_done)
                w(t, s_dv, ov)
                for i in range(NI):
                    t.matmul(pb[:, :], BA(WFB[op] + i * 128, 128),
                             BA(XL2 + i * 1024 + tc * 512, 512),
                             start=(i == 0), stop=False)
                ins = t.matmul(pb[:, :], BA(BMT + o * 128, 128),
                               gt_sb.ap()[:, tc * 512:(tc + 1) * 512],
                               start=False, stop=True)
                if tc == 1:
                    p8["wf_te"][op] = s_te.n + 1
                s_te.inc(ins, 1)
            T(_)
            f_te = s_te.n

            def _(v, o=o, tc=tc, ft=f_te, xr=p8["x2_sl_%d" % tc], os_=p8["ob_ss"][tc]):
                pb = pbE if tc == 0 else pbF
                w(v, s_te, ft)
                w(v, s_sl, xr)
                w(v, s_ss, os_)
                v.tensor_tensor(FA(YB[tc], 512), pb[:, :], FA(X2S8[tc], 512), ALU.add)
                ins = v.tensor_scalar(FA(YB[tc], 512), FA(YB[tc], 512),
                                      bcol_sb.ap()[:, o:o + 1], None, ALU.add)
                p8["o_dv"][tc] = s_dv.inc(ins, 1)
            V(_)
            o_dv = s_dv.n

            def _(sy, o=o, tc=tc, od=o_dv):
                w(sy, s_dv, od)
                ins = sy.dma_start(out.ap()[:, o * 1024 + tc * 512: o * 1024 + (tc + 1) * 512],
                                   FA(YB[tc], 512))
                p8["ob_ss"][tc] = s_ss.inc(ins, 16)
            SY(_)

    def _(sy):
        sy.wait_ge(s_ss.h, s_ss.n)
    SY(_)

    block.__exit__(None, None, None)
    return nc



def _pblk(r0, k):
    if k == 0:
        return r0
    if k == r0:
        return 0
    return k


def _phys_pos(r0):
    t = np.arange(S)
    g = t // 512
    blk = (t % 512) // 128
    off = t % 128
    tb = np.array([_pblk(r0, k) for k in range(4)])[blk]
    return g * 512 + tb * 128 + off


def _own_pos(r0):
    u = np.arange(1024)
    return 512 * (u // 128) + 128 * r0 + (u % 128)


def _rope_tables(pos, scale=1.0):
    inv = 1.0 / (10000.0 ** (np.arange(0, 64, dtype=np.float64) * 2.0 / 128.0))
    ang = pos.astype(np.float64)[None, :] * inv[:, None]
    cc = np.concatenate([np.cos(ang), np.cos(ang)], 0) * scale
    ss = np.concatenate([-np.sin(ang), np.sin(ang)], 0) * scale
    return (np.ascontiguousarray(cc.astype(np.float32)),
            np.ascontiguousarray(ss.astype(np.float32)))


def _tiles_T(wm):
    t = wm.reshape(32, 128, 32, 128)
    return np.ascontiguousarray(t.transpose(0, 3, 2, 1).reshape(32, 128, 4096))


def kernel(L0_attn_x, L0_attn_w_in, L0_attn_w_out, L0_ffn_W, L0_ffn_b,
           L0_ffn_Wq, L0_ffn_Wk, L0_ffn_A, L0_ffn_Bm,
           phases=8, debug=False):
    x = np.asarray(L0_attn_x, np.float32)
    w_in = np.asarray(L0_attn_w_in, np.float32)
    w_out = np.asarray(L0_attn_w_out, np.float32)
    W = np.asarray(L0_ffn_W, np.float32)
    b = np.asarray(L0_ffn_b, np.float32)
    Wq = np.asarray(L0_ffn_Wq, np.float32)
    Wk = np.asarray(L0_ffn_Wk, np.float32)
    Af = np.asarray(L0_ffn_A, np.float32).reshape(128, 4096)
    Bm = np.asarray(L0_ffn_Bm, np.float32)

    wq_img = _tiles_T(w_in[0:4096])
    wo_img = _tiles_T(w_out)
    wf_img = _tiles_T(W)
    wk_img = np.zeros((8, 128, 4096), np.float32)
    wv_img = np.zeros((8, 128, 4096), np.float32)
    for h in range(8):
        kw = w_in[4096 + 128 * h:4096 + 128 * (h + 1)]
        vw = w_in[5120 + 128 * h:5120 + 128 * (h + 1)]
        wk_img[h] = kw.reshape(128, 32, 128).transpose(2, 1, 0).reshape(128, 4096)
        wv_img[h] = vw.reshape(128, 32, 128).transpose(2, 1, 0).reshape(128, 4096)

    qk = np.concatenate([Wq, Wk], 0)
    gwqk_img = np.ascontiguousarray(qk.reshape(512, 32, 128).transpose(1, 2, 0))
    gwat_img = np.ascontiguousarray(Af.reshape(128, 32, 128).transpose(1, 2, 0))
    bmt_img = np.ascontiguousarray(Bm.transpose(0, 2, 1).reshape(128, 4096))
    bcol_img = np.ascontiguousarray(b.reshape(32, 128).T)
    eye_img = np.eye(128, dtype=np.float32)

    in_maps = []
    for c in range(8):
        bb, r0 = c // 4, c % 4
        pp = _phys_pos(r0)
        xT = x[bb].T
        xTp = xT[:, pp]
        xt_img = np.ascontiguousarray(
            xTp.reshape(32, 128, 8, 512).transpose(2, 1, 0, 3))

        cck, ssk = _rope_tables(pp)
        ccq, ssq = _rope_tables(_own_pos(r0), scale=1.0 / math.sqrt(128.0))

        kk = np.arange(128)[:, None]
        qq = np.arange(128)[None, :]
        m = np.zeros((128, 4, 4, 128), np.float32)
        for rr in range(4):
            vis = (128 * _pblk(r0, rr) + kk <= 128 * r0 + qq).astype(np.float32)
            for g4 in range(4):
                m[:, rr, g4, :] = vis
        msk_img = np.ascontiguousarray(m.reshape(128, 2048))

        in_maps.append(dict(
            xt=xt_img, wq=wq_img, wk=wk_img, wv=wv_img, wo=wo_img, wf=wf_img,
            gwqk_i=gwqk_img, gwat_i=gwat_img, bmt_i=bmt_img, bcol_i=bcol_img,
            cck_i=cck, ssk_i=ssk, ccq_i=ccq, ssq_i=ssq, msk_i=msk_img,
            eye_i=eye_img,
        ))

    nc = build_nc(phases=phases, debug=debug)
    res = run_bass_kernel_spmd(nc, in_maps, core_ids=list(range(8)))

    if debug:
        return res

    yfull = np.empty((B, S, E), np.float32)
    for c in range(8):
        bb, r0 = c // 4, c % 4
        r = np.asarray(res.results[c]["out"], np.float32)
        yc = r.reshape(128, 32, 1024).transpose(2, 1, 0).reshape(1024, 4096)
        yfull[bb, _own_pos(r0), :] = yc
    return yfull


# version 08
# speedup vs baseline: 1.1814x; 1.1712x over previous
import math
import numpy as np
import concourse.bass as bass
import concourse.mybir as mybir
from concourse.bass_utils import run_bass_kernel_spmd

F32 = mybir.dt.float32
BF16 = mybir.dt.bfloat16
AF = mybir.ActivationFunctionType
ALU = mybir.AluOpType
AX = mybir.AxisListType

B, S, E = 2, 4096, 4096
NI = 32
NH = 8
INVSQ_DK = float(1.0 / np.sqrt(np.float32(32.0)))
LN_EPS = 1e-5
EXP_SHIFT = -10.0

TB_CCK = 0
TB_SSK = 4096
TB_CCQ = 8192
TB_SSQ = 9216
TB_MSK = 10240
WRK = 12288
BA_COLS = 67584


def build_nc(phases=8, debug=False):
    nc = bass.Bass("TRN2", target_bir_lowering=False)
    dt = nc.dram_tensor

    xt = dt("xt", [8, 128, NI, 512], F32, kind="ExternalInput")
    wq = dt("wq", [32, 128, 4096], F32, kind="ExternalInput")
    wk = dt("wk", [8, 128, 4096], F32, kind="ExternalInput")
    wv = dt("wv", [8, 128, 4096], F32, kind="ExternalInput")
    wo = dt("wo", [32, 128, 4096], F32, kind="ExternalInput")
    wf = dt("wf", [32, 128, 4096], F32, kind="ExternalInput")
    gwqk_i = dt("gwqk_i", [32, 128, 512], F32, kind="ExternalInput")
    gwat_i = dt("gwat_i", [32, 128, 128], F32, kind="ExternalInput")
    bmt_i = dt("bmt_i", [128, 4096], F32, kind="ExternalInput")
    bcol_i = dt("bcol_i", [128, 32], F32, kind="ExternalInput")
    cck_i = dt("cck_i", [128, 4096], F32, kind="ExternalInput")
    ssk_i = dt("ssk_i", [128, 4096], F32, kind="ExternalInput")
    ccq_i = dt("ccq_i", [128, 1024], F32, kind="ExternalInput")
    ssq_i = dt("ssq_i", [128, 1024], F32, kind="ExternalInput")
    msk_i = dt("msk_i", [128, 2048], F32, kind="ExternalInput")
    eye_i = dt("eye_i", [128, 128], F32, kind="ExternalInput")
    out = dt("out", [128, 32768], F32, kind="ExternalOutput")

    dkind = "ExternalOutput" if debug else "Internal"
    oxln = dt("oxln", [128, 32768], BF16, kind=dkind)
    kd = dt("kd", [128, 32768], BF16, kind=dkind)
    vd = dt("vd", [128, 32768], BF16, kind=dkind)
    qd = dt("qd", [128, 32768], BF16, kind=dkind)
    aod = dt("aod", [128, 32768], BF16, kind=dkind)
    x2d = dt("x2d", [128, 32768], F32, kind=dkind)
    xl2d = dt("xl2d", [128, 32768], BF16, kind=dkind)
    gtd = dt("gtd", [128, 1024], BF16, kind=dkind)
    qkd = dt("qkd", [128, 640], F32, kind=dkind)
    gsd = dt("gsd", [128, 32], F32, kind=dkind)

    ba = nc.alloc_sbuf_tensor("ba", [128, BA_COLS], BF16)
    fa = nc.alloc_sbuf_tensor("fa", [128, 4608], F32)
    ones_c = nc.alloc_sbuf_tensor("ones_c", [128, 1], BF16)
    ones_cf = nc.alloc_sbuf_tensor("ones_cf", [128, 1], F32)
    ones_r = nc.alloc_sbuf_tensor("ones_r", [1, 128], BF16)
    epsc = nc.alloc_sbuf_tensor("epsc", [1, 1], F32)
    expb = nc.alloc_sbuf_tensor("expb", [128, 1], F32)
    bcol_sb = nc.alloc_sbuf_tensor("bcol_sb", [128, 32], F32)
    eyebf = nc.alloc_sbuf_tensor("eyebf", [128, 128], BF16)
    rows = nc.alloc_sbuf_tensor("rows", [1, 2048], F32)
    rows2 = nc.alloc_sbuf_tensor("rows2", [1, 2048], F32)
    rowsb = nc.alloc_sbuf_tensor("rowsb", [1, 2048], BF16)
    bcr = nc.alloc_sbuf_tensor("bcr", [128, 512], F32)
    bcmr = nc.alloc_sbuf_tensor("bcmr", [128, 512], F32)
    bcr2 = nc.alloc_sbuf_tensor("bcr2", [128, 1024], F32)
    bcmr2 = nc.alloc_sbuf_tensor("bcmr2", [128, 1024], F32)
    recb = nc.alloc_sbuf_tensor("recb", [128, 512], F32)
    gsml = nc.alloc_sbuf_tensor("gsml", [128, 32], F32)
    qsb = nc.alloc_sbuf_tensor("qsb", [128, 512], F32)
    hsb = nc.alloc_sbuf_tensor("hsb", [128, 128], F32)
    s8b = nc.alloc_sbuf_tensor("s8b", [128, 256], F32)
    gbf = nc.alloc_sbuf_tensor("gbf", [128, 128], BF16)
    gt_sb = nc.alloc_sbuf_tensor("gt_sb", [128, 1024], BF16)

    psA = nc.alloc_psum_tensor("psA", [128, 2048], F32).ap()
    pbE = nc.alloc_psum_tensor("pbE", [128, 512], F32).ap()
    pbF = nc.alloc_psum_tensor("pbF", [128, 512], F32).ap()
    pbD = nc.alloc_psum_tensor("pbD", [128, 512], F32).ap()
    psTb = nc.alloc_psum_tensor("psTb", [128, 1024], BF16).ap()

    def BA(a, n):
        return ba.ap()[:, a:a + n]

    def FA(a, n):
        return fa.ap()[:, a:a + n]

    class Sem:
        def __init__(self, name):
            self.h = nc.semaphore(name).__enter__()
            self.n = 0

        def inc(self, instr, k):
            instr.then_inc(self.h, k)
            self.n += k
            return self.n

    s_sl = Sem("s_sl")
    s_ss = Sem("s_ss")
    s_gl = Sem("s_gl")
    s_te = Sem("s_te")
    s_ac = Sem("s_ac")
    s_dv = Sem("s_dv")
    ALLS = [s_sl, s_ss, s_gl, s_te, s_ac, s_dv]

    block = nc.Block().__enter__()

    def T(fn):
        block.tensor(fn)

    def A(fn):
        block.scalar(fn)

    def V(fn):
        block.vector(fn)

    def GP(fn):
        block.gpsimd(fn)

    def SY(fn):
        block.sync(fn)

    def w(eng, sem, n):
        if n > 0:
            eng.wait_ge(sem.h, n)

    def fence():
        snap = [(s, s.n) for s in ALLS]

        def mk(eng):
            for s, n in snap:
                w(eng, s, n)
        T(mk)
        V(mk)
        A(mk)
        GP(mk)
        SY(mk)

    def _(g):
        g.dma_start(BA(TB_CCK, 4096), cck_i.ap()).then_inc(s_gl.h, 16)
        g.dma_start(BA(TB_SSK, 4096), ssk_i.ap()).then_inc(s_gl.h, 16)
        g.dma_start(BA(TB_CCQ, 1024), ccq_i.ap()).then_inc(s_gl.h, 16)
        g.dma_start(BA(TB_SSQ, 1024), ssq_i.ap()).then_inc(s_gl.h, 16)
        g.dma_start(BA(TB_MSK, 2048), msk_i.ap()).then_inc(s_gl.h, 16)
        g.dma_start(eyebf.ap(), eye_i.ap()).then_inc(s_gl.h, 16)
        s_gl.n += 96
    GP(_)

    def _(sy):
        sy.dma_start(bcol_sb.ap(), bcol_i.ap()).then_inc(s_sl.h, 16)
        s_sl.n += 16
    SY(_)

    def _(v):
        v.memset(ones_c.ap(), 1.0)
        v.memset(ones_cf.ap(), 1.0)
        v.memset(ones_r.ap(), 1.0)
        v.memset(epsc.ap(), LN_EPS)
        ins = v.memset(expb.ap(), EXP_SHIFT)
        s_dv.inc(ins, 1)
    V(_)
    fence()

    def _(a):
        a.preload_act_table(AF.Sqrt)
    A(_)
    XG = [WRK, WRK + 16384]
    WKB = [WRK + 32768, WRK + 36864]
    WVB = [WRK + 40960, WRK + 45056]
    KEV = [WRK + 49152, WRK + 49664]
    VEVF = [WRK + 50176, WRK + 50688]
    VEVT = [WRK + 51200, WRK + 51712]
    SQB = [WRK + 52224, WRK + 52736]

    def xgv(par, i=None):
        if i is None:
            return BA(XG[par], 16384)
        return BA(XG[par] + i * 512, 512)

    st = dict(xg_te=[0, 0], xg_ss=[0, 0], kev_ss=[0, 0], vevf_dv=[0, 0],
              vevt_ss=[0, 0], kv_dv=0, vf_te=[0, 0], pstb_dv=[0, 0],
              wk_te=[0, 0], bc_dv=0, fin_dv=0, last_norm_dv=0)

    for pair in range(4):
        glist = [2 * pair, 2 * pair + 1]
        norm_dv_g = [0, 0]
        for par, g in enumerate(glist):
            def _(gp, g=g, par=par):
                w(gp, s_te, st["xg_te"][par])
                w(gp, s_ss, st["xg_ss"][par])
                flat = xt.ap()[g].rearrange("p i t -> p (i t)")
                for q4 in range(4):
                    ins = gp.dma_start(BA(XG[par] + q4 * 4096, 4096),
                                       flat[:, q4 * 4096:(q4 + 1) * 4096])
                    s_gl.inc(ins, 16)
            GP(_)
            xg_rdy = s_gl.n

            te0 = s_te.n
            ac0 = s_ac.n
            for i in range(NI):
                def _(v, i=i, par=par, xr=xg_rdy, te0=te0):
                    if i % 8 == 0:
                        w(v, s_gl, xr - 48 + 16 * (i // 8))
                    if i >= 2:
                        w(v, s_te, te0 + i - 1)
                    ins = v.tensor_tensor(BA(SQB[i % 2], 512), xgv(par, i), xgv(par, i), ALU.mult)
                    s_ac.inc(ins, 1)
                V(_)

                def _(t, i=i, par=par, xr=xg_rdy, ac0=ac0, fp=st["fin_dv"]):
                    if i % 8 == 0:
                        w(t, s_gl, xr - 48 + 16 * (i // 8))
                    if i == 0:
                        w(t, s_dv, fp)
                    t.matmul(psA[0:1, 0:512], ones_c.ap(), xgv(par, i),
                             start=(i == 0), stop=(i == NI - 1))
                    w(t, s_ac, ac0 + i + 1)
                    ins = t.matmul(psA[0:1, 512:1024], ones_c.ap(), BA(SQB[i % 2], 512),
                                   start=(i == 0), stop=(i == NI - 1))
                    s_te.inc(ins, 1)
                T(_)
            stats_te = s_te.n

            def _(v, sd=stats_te):
                w(v, s_te, sd)
                v.tensor_scalar(rows.ap()[:, 0:512], psA[0:1, 0:512],
                                1.0 / 4096.0, None, ALU.mult)
                v.tensor_scalar(rows.ap()[:, 1536:2048], psA[0:1, 512:1024],
                                1.0 / 4096.0, None, ALU.mult)
                v.tensor_tensor(rows.ap()[:, 512:1024], rows.ap()[:, 0:512],
                                rows.ap()[:, 0:512], ALU.mult)
                ins = v.tensor_tensor(rows.ap()[:, 1536:2048], rows.ap()[:, 1536:2048],
                                      rows.ap()[:, 512:1024], ALU.subtract)
                s_dv.inc(ins, 1)
            V(_)
            st["fin_dv"] = s_dv.n

            def _(a, vd_=s_dv.n):
                w(a, s_dv, vd_)
                ins = a.activation(rows.ap()[:, 512:1024], rows.ap()[:, 1536:2048],
                                   AF.Sqrt, bias=epsc.ap())
                s_ac.inc(ins, 1)
            A(_)
            sd_ac = s_ac.n

            def _(v, sd=sd_ac):
                w(v, s_ac, sd)
                v.reciprocal(rows.ap()[:, 512:1024], rows.ap()[:, 512:1024])
                v.tensor_tensor(rows.ap()[:, 1024:1536], rows.ap()[:, 0:512],
                                rows.ap()[:, 512:1024], ALU.mult)
                v.tensor_copy(rowsb.ap()[:, 0:512], rows.ap()[:, 512:1024])
                ins = v.tensor_copy(rowsb.ap()[:, 512:1024], rows.ap()[:, 1024:1536])
                s_dv.inc(ins, 1)
            V(_)
            rows_dv = s_dv.n

            def _(t, rd=rows_dv, ba_=st["bc_dv"]):
                w(t, s_dv, rd)
                w(t, s_dv, ba_)
                t.matmul(psA[:, 1024:1536], ones_r.ap(), rowsb.ap()[:, 0:512],
                         start=True, stop=True)
                ins = t.matmul(psA[:, 1536:2048], ones_r.ap(), rowsb.ap()[:, 512:1024],
                               start=True, stop=True)
                s_te.inc(ins, 1)
            T(_)
            bc_te = s_te.n

            def _(v, par=par, bt=bc_te):
                w(v, s_te, bt)
                v.tensor_copy(bcr.ap(), psA[:, 1024:1536])
                ins = v.tensor_copy(bcmr.ap(), psA[:, 1536:2048])
                st["bc_dv"] = s_dv.inc(ins, 1)
                for i in range(NI):
                    v.tensor_tensor(xgv(par, i), xgv(par, i), bcr.ap(), ALU.mult)
                    ins = v.tensor_tensor(xgv(par, i), xgv(par, i), bcmr.ap(), ALU.subtract)
                s_dv.inc(ins, 1)
            V(_)
            norm_dv_g[par] = s_dv.n
            st["last_norm_dv"] = s_dv.n

            def _(sy, g=g, par=par, nd=s_dv.n):
                w(sy, s_dv, nd)
                dst = oxln.ap().rearrange("p (i g q) -> p i g q", g=8, q=128)[:, :, g, :]
                src = BA(XG[par], 16384).rearrange("p (i t) -> p i t", t=512)[:, :, 0:128]
                ins = sy.dma_start(dst, src)
                st["xg_ss"][par] = s_ss.inc(ins, 16)
            SY(_)

        for h in range(NH):
            hp = h % 2

            def _(gp, h=h, hp=hp):
                w(gp, s_te, st["wk_te"][hp])
                gp.dma_start(BA(WKB[hp], 4096), wk.ap()[h]).then_inc(s_gl.h, 16)
                ins = gp.dma_start(BA(WVB[hp], 4096), wv.ap()[h])
                s_gl.n += 16
                s_gl.inc(ins, 16)
            GP(_)
            wkv_rdy = s_gl.n

            for par, g in enumerate(glist):
                def _(t, hp=hp, par=par, wr=wkv_rdy, nd=norm_dv_g[par], kv=st["kv_dv"]):
                    w(t, s_gl, wr)
                    w(t, s_dv, nd)
                    w(t, s_dv, kv)
                    for i in range(NI):
                        ins = t.matmul(pbE[:, :], BA(WKB[hp] + i * 128, 128), xgv(par, i),
                                       start=(i == 0), stop=(i == NI - 1))
                    s_te.inc(ins, 1)
                T(_)
                k_te = s_te.n

                def _(v, g=g, par=par, kt=k_te, ks=st["kev_ss"][par]):
                    w(v, s_te, kt)
                    w(v, s_ss, ks)
                    v.tensor_copy(FA(0, 512)[0:64, :], pbE[64:128, :])
                    v.tensor_copy(FA(0, 512)[64:128, :], pbE[0:64, :])
                    v.tensor_tensor(FA(512, 512), pbE[:, :], BA(TB_CCK + g * 512, 512), ALU.mult)
                    v.tensor_tensor(FA(0, 512), FA(0, 512), BA(TB_SSK + g * 512, 512), ALU.mult)
                    ins = v.tensor_tensor(BA(KEV[par], 512), FA(512, 512), FA(0, 512), ALU.add)
                    st["kv_dv"] = s_dv.inc(ins, 1)
                V(_)

                def _(sy, h=h, g=g, par=par, kdv=st["kv_dv"]):
                    w(sy, s_dv, kdv)
                    ins = sy.dma_start(kd.ap()[:, h * 4096 + g * 512: h * 4096 + (g + 1) * 512],
                                       BA(KEV[par], 512))
                    st["kev_ss"][par] = s_ss.inc(ins, 16)
                SY(_)

                def _(t, hp=hp, par=par, va=st["vevf_dv"][par]):
                    w(t, s_dv, va)
                    for i in range(NI):
                        ins = t.matmul(pbF[:, :], BA(WVB[hp] + i * 128, 128), xgv(par, i),
                                       start=(i == 0), stop=(i == NI - 1))
                    s_te.inc(ins, 1)
                T(_)
                v_te = s_te.n

                def _(v, vt=v_te, par=par, ft=st["vf_te"][par]):
                    w(v, s_te, vt)
                    w(v, s_te, ft)
                    ins = v.tensor_copy(BA(VEVF[par], 512), pbF[:, :])
                    st["vevf_dv"][par] = s_dv.inc(ins, 1)
                V(_)
                vf_dv = s_dv.n

                def _(t, par=par, va=vf_dv, pa=st["pstb_dv"][par]):
                    w(t, s_dv, va)
                    w(t, s_dv, pa)
                    for ts in range(4):
                        ins = t.transpose(psTb[:, par * 512 + ts * 128: par * 512 + (ts + 1) * 128],
                                          BA(VEVF[par] + ts * 128, 128), eyebf.ap())
                    st["vf_te"][par] = s_te.inc(ins, 1)
                T(_)
                tr_te = s_te.n

                def _(v, par=par, tt=tr_te, vs=st["vevt_ss"][par]):
                    w(v, s_te, tt)
                    w(v, s_ss, vs)
                    ins = v.tensor_copy(BA(VEVT[par], 512), psTb[:, par * 512:(par + 1) * 512])
                    st["pstb_dv"][par] = s_dv.inc(ins, 1)
                V(_)
                vt_dv = s_dv.n

                def _(sy, h=h, g=g, par=par, va=vt_dv):
                    w(sy, s_dv, va)
                    ins = sy.dma_start(vd.ap()[:, h * 4096 + g * 512: h * 4096 + (g + 1) * 512],
                                       BA(VEVT[par], 512))
                    st["vevt_ss"][par] = s_ss.inc(ins, 16)
                SY(_)
            st["wk_te"][hp] = s_te.n
        st["xg_te"][0] = s_te.n
        st["xg_te"][1] = s_te.n

    fence()
    if phases <= 1:
        def _(sy):
            sy.wait_ge(s_ss.h, s_ss.n)
        SY(_)
        block.__exit__(None, None, None)
        return nc

    OXB = WRK
    WQB = [WRK + 32768, WRK + 36864]
    QEV = [WRK + 40960, WRK + 41472, WRK + 41984, WRK + 42496]

    def _(sy):
        ins = sy.dma_start(BA(OXB, 32768), oxln.ap())
        s_sl.inc(ins, 16)
    SY(_)
    ox_rdy = s_sl.n

    p2 = dict(wq_te=[0, 0], q_dv=[0, 0], qev_ss=[0, 0, 0, 0])
    for f in range(32):
        fp = f % 2
        h, g4 = f // 4, f % 4

        def _(gp, f=f, fp=fp):
            w(gp, s_te, p2["wq_te"][fp])
            ins = gp.dma_start(BA(WQB[fp], 4096), wq.ap()[f])
            s_gl.inc(ins, 16)
        GP(_)
        wq_rdy = s_gl.n

        for half in range(2):
            def _(t, f=f, fp=fp, half=half, wr=wq_rdy, orr=ox_rdy, qv=p2["q_dv"][half]):
                pb = pbE if half == 0 else pbF
                if f == 0 and half == 0:
                    w(t, s_sl, orr)
                w(t, s_gl, wr)
                w(t, s_dv, qv)
                for i in range(NI):
                    ins = t.matmul(pb[:, :], BA(WQB[fp] + i * 128, 128),
                                   BA(OXB + i * 1024 + half * 512, 512),
                                   start=(i == 0), stop=(i == NI - 1))
                if half == 1:
                    p2["wq_te"][fp] = s_te.n + 1
                s_te.inc(ins, 1)
            T(_)
            q_te = s_te.n

            qe = 2 * fp + half

            def _(v, half=half, qt=q_te, qe=qe, qs=p2["qev_ss"][qe]):
                pb = pbE if half == 0 else pbF
                w(v, s_te, qt)
                w(v, s_ss, qs)
                v.tensor_copy(FA(0, 512)[0:64, :], pb[64:128, :])
                v.tensor_copy(FA(0, 512)[64:128, :], pb[0:64, :])
                v.tensor_tensor(FA(512, 512), pb[:, :], BA(TB_CCQ + half * 512, 512), ALU.mult)
                v.tensor_tensor(FA(0, 512), FA(0, 512), BA(TB_SSQ + half * 512, 512), ALU.mult)
                ins = v.tensor_tensor(BA(QEV[qe], 512), FA(512, 512), FA(0, 512), ALU.add)
                p2["q_dv"][half] = s_dv.inc(ins, 1)
            V(_)

            def _(sy, h=h, g4=g4, half=half, qe=qe, qdv=p2["q_dv"][half]):
                w(sy, s_dv, qdv)
                dst = qd.ap().rearrange("p (hh j g q) -> p hh j g q", hh=8, j=8, g=4)[
                    :, h, 4 * half:4 * half + 4, g4, :]
                src = BA(QEV[qe], 512).rearrange("p (j q) -> p j q", j=4)
                ins = sy.dma_start(dst, src)
                p2["qev_ss"][qe] = s_ss.inc(ins, 16)
            SY(_)

    fence()
    if phases <= 2:
        def _(sy):
            sy.wait_ge(s_ss.h, s_ss.n)
        SY(_)
        block.__exit__(None, None, None)
        return nc

    def _(a):
        a.preload_act_table(AF.Exp)
    A(_)
    KHB = [WRK, WRK + 4096]
    VBF = [WRK + 8192, WRK + 12288]
    QTB = [WRK + 16384, WRK + 16896]
    PRG = [WRK + 17408, WRK + 19456]
    AO = WRK + 21504

    a3 = dict(kv_te=[0, 0], qt_te=[0, 0], prg_te=[0, 0],
              pbO_dv=[0, 0], den_dv=0, rb_dv=0,
              psA_ac_0=0, psA_ac_1=0)

    for h in range(NH):
        par = h % 2

        def _(sy, h=h, par=par, kt=a3["kv_te"][par]):
            w(sy, s_te, kt)
            sy.dma_start(BA(KHB[par], 4096),
                         kd.ap()[:, h * 4096:(h + 1) * 4096]).then_inc(s_sl.h, 16)
            ins = sy.dma_start(BA(VBF[par], 4096),
                               vd.ap()[:, h * 4096:(h + 1) * 4096])
            s_sl.n += 16
            s_sl.inc(ins, 16)
        SY(_)
        kv_rdy = s_sl.n

        for j in range(NH):
            jp = j % 2

            def _(sy, h=h, j=j, jp=jp, qt=a3["qt_te"][jp]):
                w(sy, s_te, qt)
                ins = sy.dma_start(BA(QTB[jp], 512),
                                   qd.ap()[:, h * 4096 + j * 512: h * 4096 + (j + 1) * 512])
                s_sl.inc(ins, 16)
            SY(_)
            qt_rdy = s_sl.n

            for c2 in range(j + 1):
                pp = c2 % 2

                for bk in range(2):
                    def _(t, j=j, jp=jp, c2=c2, par=par, bk=bk, qr=qt_rdy, kr=kv_rdy,
                          pa=a3["psA_ac_%d" % bk], po=a3["pbO_dv"][jp],
                          rd=a3.get("rb_dv", 0)):
                        if c2 == 0 and bk == 0:
                            w(t, s_sl, qr)
                            w(t, s_sl, kr)
                            w(t, s_dv, po)
                            if bk == 0:
                                w(t, s_dv, rd)
                        w(t, s_ac, pa)
                        for rr in (0, 1):
                            kt = c2 * 4 + 2 * bk + rr
                            ins = t.matmul(psA[:, (2 * bk + rr) * 512:(2 * bk + rr + 1) * 512],
                                           BA(KHB[par] + kt * 128, 128), BA(QTB[jp], 512),
                                           start=True, stop=True)
                        s_te.inc(ins, 1)
                    T(_)
                    sc_te = s_te.n

                    def _(a, pp=pp, bk=bk, st_=sc_te, pt=a3["prg_te"][pp]):
                        w(a, s_te, st_)
                        if bk == 0:
                            w(a, s_te, pt)
                        ins = a.activation(BA(PRG[pp] + bk * 1024, 1024),
                                           psA[:, bk * 1024:(bk + 1) * 1024],
                                           AF.Exp, bias=expb.ap())
                        a3["psA_ac_%d" % bk] = s_ac.inc(ins, 1)
                    A(_)
                exp_ac = s_ac.n

                mask_dv = 0
                if c2 == j:
                    def _(v, pp=pp, ea=exp_ac):
                        w(v, s_ac, ea)
                        ins = v.tensor_tensor(BA(PRG[pp], 2048), BA(PRG[pp], 2048),
                                              BA(TB_MSK, 2048), ALU.mult)
                        s_dv.inc(ins, 1)
                    V(_)
                    mask_dv = s_dv.n

                def _(t, j=j, jp=jp, c2=c2, par=par, pp=pp, ea=exp_ac, md=mask_dv,
                      dd=a3["den_dv"]):
                    pbO = pbE if jp == 0 else pbF
                    if md:
                        w(t, s_dv, md)
                    else:
                        w(t, s_ac, ea)
                    if c2 == 0:
                        w(t, s_dv, dd)
                    for rr in range(4):
                        kt = c2 * 4 + rr
                        t.matmul(pbD[0:1, :], ones_c.ap(), BA(PRG[pp] + rr * 512, 512),
                                 start=(c2 == 0 and rr == 0), stop=(c2 == j and rr == 3))
                        ins = t.matmul(pbO[:, :], BA(VBF[par] + kt * 128, 128),
                                       BA(PRG[pp] + rr * 512, 512),
                                       start=(c2 == 0 and rr == 0), stop=(c2 == j and rr == 3))
                    a3["prg_te"][pp] = s_te.inc(ins, 1)
                T(_)
            den_te = s_te.n
            a3["qt_te"][jp] = s_te.n

            def _(v, dt_=den_te, rt=a3.get("rb_te", 0)):
                w(v, s_te, dt_)
                w(v, s_te, rt)
                v.reciprocal(rows.ap()[:, 0:512], pbD[0:1, :])
                ins = v.tensor_copy(rowsb.ap()[:, 0:512], rows.ap()[:, 0:512])
                a3["den_dv"] = s_dv.inc(ins, 1)
            V(_)
            den_dv = s_dv.n

            def _(t, dd=den_dv, pa=a3["psA_ac_0"]):
                w(t, s_dv, dd)
                w(t, s_ac, pa)
                ins = t.matmul(psA[:, 0:512], ones_r.ap(), rowsb.ap()[:, 0:512],
                               start=True, stop=True)
                a3["rb_te"] = s_te.inc(ins, 1)
            T(_)
            rb_te = s_te.n

            def _(v, h=h, j=j, jp=jp, rt=rb_te):
                pbO = pbE if jp == 0 else pbF
                w(v, s_te, rt)
                v.tensor_copy(recb.ap(), psA[:, 0:512])
                ins = v.engine_nop()
                a3["rb_dv"] = s_dv.inc(ins, 1)
                dst = BA(AO, 32768).rearrange("p (e u) -> p e u", u=1024)[
                    :, 4 * h:4 * h + 4, j * 128:(j + 1) * 128]
                ins = v.tensor_tensor(dst, pbO.rearrange("p (g q) -> p g q", g=4),
                                      recb.ap().rearrange("p (g q) -> p g q", g=4), ALU.mult)
                a3["pbO_dv"][jp] = s_dv.inc(ins, 1)
            V(_)
        a3["kv_te"][par] = s_te.n
    ao_dv = s_dv.n

    if debug:
        def _(sy, ad=ao_dv):
            w(sy, s_dv, ad)
            ins = sy.dma_start(aod.ap(), BA(AO, 32768))
            s_ss.inc(ins, 16)
        SY(_)

    fence()
    if phases <= 3:
        def _(sy):
            sy.wait_ge(s_ss.h, s_ss.n)
        SY(_)
        block.__exit__(None, None, None)
        return nc

    WOB = [WRK, WRK + 4096]
    XRS = [0, 512]
    X2B = [1024, 1536]
    SQ2 = [2048, 2560]

    p5 = dict(wo_te=[0, 0], x2_dv=[0, 0], st_te=[0, 0], x2_ss=[0, 0])
    for o in range(32):
        op = o % 2

        def _(gp, o=o, op=op, wt_=p5["wo_te"][op]):
            w(gp, s_te, wt_)
            ins = gp.dma_start(BA(WOB[op], 4096), wo.ap()[o])
            s_gl.inc(ins, 16)
        GP(_)
        wo_rdy = s_gl.n

        for tc in range(2):
            def _(sy, o=o, tc=tc, xd=p5["x2_dv"][tc]):
                w(sy, s_dv, xd)
                src = xt.ap().rearrange("g p i t -> p g i t")[:, 4 * tc:4 * tc + 4, o, 0:128]
                ins = sy.dma_start(FA(XRS[tc], 512).rearrange("p (g q) -> p g q", g=4), src)
                p5["xr_sl_%d" % tc] = s_sl.inc(ins, 16)
            SY(_)

            def _(t, o=o, op=op, tc=tc, wr=wo_rdy, xv=p5["x2_dv"][tc]):
                pb = pbE if tc == 0 else pbF
                w(t, s_gl, wr)
                w(t, s_dv, xv)
                for e in range(NI):
                    ins = t.matmul(pb[:, :], BA(WOB[op] + e * 128, 128),
                                   BA(AO + e * 1024 + tc * 512, 512),
                                   start=(e == 0), stop=(e == NI - 1))
                if tc == 1:
                    p5["wo_te"][op] = s_te.n + 1
                s_te.inc(ins, 1)
            T(_)
            y_te = s_te.n

            def _(v, tc=tc, yt=y_te, xr=p5["xr_sl_%d" % tc], xs=p5["x2_ss"][tc],
                  sq=p5["st_te"][tc]):
                pb = pbE if tc == 0 else pbF
                w(v, s_te, yt)
                w(v, s_sl, xr)
                w(v, s_ss, xs)
                w(v, s_te, sq)
                v.tensor_tensor(FA(X2B[tc], 512), pb[:, :], FA(XRS[tc], 512), ALU.add)
                ins = v.tensor_tensor(FA(SQ2[tc], 512), FA(X2B[tc], 512),
                                      FA(X2B[tc], 512), ALU.mult)
                p5["x2_dv"][tc] = s_dv.inc(ins, 1)
            V(_)
            x2_dv = s_dv.n

            def _(sy, o=o, tc=tc, xd=x2_dv):
                w(sy, s_dv, xd)
                ins = sy.dma_start(x2d.ap()[:, o * 1024 + tc * 512: o * 1024 + (tc + 1) * 512],
                                   FA(X2B[tc], 512))
                p5["x2_ss"][tc] = s_ss.inc(ins, 16)
            SY(_)

            def _(t, o=o, tc=tc, xd=x2_dv):
                w(t, s_dv, xd)
                t.matmul(psA[0:1, tc * 512:(tc + 1) * 512], ones_cf.ap(), FA(X2B[tc], 512),
                         start=(o == 0), stop=(o == 31))
                ins = t.matmul(psA[0:1, 1024 + tc * 512:1024 + (tc + 1) * 512], ones_cf.ap(),
                               FA(SQ2[tc], 512), start=(o == 0), stop=(o == 31))
                p5["st_te"][tc] = s_te.inc(ins, 1)
            T(_)
    st5_te = s_te.n

    def _(v, sd=st5_te):
        w(v, s_te, sd)
        v.tensor_scalar(rows2.ap()[:, 0:1024], psA[0:1, 0:1024], 1.0 / 4096.0, None, ALU.mult)
        v.tensor_scalar(rows.ap()[:, 1024:2048], psA[0:1, 1024:2048], 1.0 / 4096.0, None, ALU.mult)
        v.tensor_tensor(rows2.ap()[:, 1024:2048], rows2.ap()[:, 0:1024],
                        rows2.ap()[:, 0:1024], ALU.mult)
        ins = v.tensor_tensor(rows.ap()[:, 1024:2048], rows.ap()[:, 1024:2048],
                              rows2.ap()[:, 1024:2048], ALU.subtract)
        s_dv.inc(ins, 1)
    V(_)
    v2_dv = s_dv.n

    def _(a, vd_=v2_dv):
        a.preload_act_table(AF.Sqrt)
        w(a, s_dv, vd_)
        ins = a.activation(rows2.ap()[:, 1024:2048], rows.ap()[:, 1024:2048],
                           AF.Sqrt, bias=epsc.ap())
        s_ac.inc(ins, 1)
    A(_)
    sd2_ac = s_ac.n

    def _(v, sd=sd2_ac):
        w(v, s_ac, sd)
        v.reciprocal(rows2.ap()[:, 1024:2048], rows2.ap()[:, 1024:2048])
        v.tensor_tensor(rows.ap()[:, 0:1024], rows2.ap()[:, 0:1024],
                        rows2.ap()[:, 1024:2048], ALU.mult)
        v.tensor_copy(rowsb.ap()[:, 0:1024], rows2.ap()[:, 1024:2048])
        ins = v.tensor_copy(rowsb.ap()[:, 1024:2048], rows.ap()[:, 0:1024])
        s_dv.inc(ins, 1)
    V(_)
    r2_dv = s_dv.n

    def _(t, rd=r2_dv):
        w(t, s_dv, rd)
        t.matmul(psA[:, 0:512], ones_r.ap(), rowsb.ap()[:, 0:512], start=True, stop=True)
        t.matmul(psA[:, 512:1024], ones_r.ap(), rowsb.ap()[:, 512:1024], start=True, stop=True)
        t.matmul(psA[:, 1024:1536], ones_r.ap(), rowsb.ap()[:, 1024:1536], start=True, stop=True)
        ins = t.matmul(psA[:, 1536:2048], ones_r.ap(), rowsb.ap()[:, 1536:2048],
                       start=True, stop=True)
        s_te.inc(ins, 1)
    T(_)
    bc2_te = s_te.n

    def _(v, bt=bc2_te):
        w(v, s_te, bt)
        v.tensor_copy(bcr2.ap(), psA[:, 0:1024])
        ins = v.tensor_copy(bcmr2.ap(), psA[:, 1024:2048])
        s_dv.inc(ins, 1)
    V(_)
    fence()
    if phases <= 5:
        def _(sy):
            sy.wait_ge(s_ss.h, s_ss.n)
        SY(_)
        block.__exit__(None, None, None)
        return nc

    XL2 = WRK + 2048
    X2S = [3072, 3584]

    p6 = dict(xl_dv=[0, 0])
    for i in range(NI):
        for tc in range(2):
            pp = (2 * i + tc) % 2

            def _(sy, i=i, tc=tc, pp=pp, xd=p6["xl_dv"][pp]):
                w(sy, s_dv, xd)
                ins = sy.dma_start(FA(X2S[pp], 512),
                                   x2d.ap()[:, i * 1024 + tc * 512: i * 1024 + (tc + 1) * 512])
                s_sl.inc(ins, 16)
            SY(_)
            x2_rdy = s_sl.n

            def _(v, i=i, tc=tc, pp=pp, xr=x2_rdy):
                w(v, s_sl, xr)
                v.tensor_tensor(FA(X2S[pp], 512), FA(X2S[pp], 512),
                                bcr2.ap()[:, tc * 512:(tc + 1) * 512], ALU.mult)
                ins = v.tensor_tensor(BA(XL2 + i * 1024 + tc * 512, 512), FA(X2S[pp], 512),
                                      bcmr2.ap()[:, tc * 512:(tc + 1) * 512], ALU.subtract)
                p6["xl_dv"][pp] = s_dv.inc(ins, 1)
            V(_)
    xl2_dv = s_dv.n

    if debug:
        def _(sy, xd=xl2_dv):
            w(sy, s_dv, xd)
            ins = sy.dma_start(xl2d.ap(), BA(XL2, 32768))
            s_ss.inc(ins, 16)
        SY(_)
    fence()

    def _(a):
        a.preload_act_table(AF.Exp)
    A(_)
    GWQ = WRK + 34816
    GWA = WRK + 51200

    def _(gp):
        gp.dma_start(BA(GWQ, 16384).rearrange("p (i n) -> p i n", n=512),
                     gwqk_i.ap().rearrange("i p n -> p i n")).then_inc(s_gl.h, 16)
        ins = gp.dma_start(BA(GWA, 4096).rearrange("p (i n) -> p i n", n=128),
                           gwat_i.ap().rearrange("i p n -> p i n"))
        s_gl.n += 16
        s_gl.inc(ins, 16)
    GP(_)
    gw_rdy = s_gl.n

    p7 = dict(qk_dv=0, h_dv=0, pt_dv=0, gb_te=0)
    for ts in range(8):
        def _(t, ts=ts, gr=gw_rdy, qv=p7["qk_dv"], hv=p7["h_dv"]):
            if ts == 0:
                w(t, s_gl, gr)
            w(t, s_dv, qv)
            for i in range(NI):
                ins = t.matmul(psA[:, 0:512], BA(XL2 + i * 1024 + ts * 128, 128),
                               BA(GWQ + i * 512, 512), start=(i == 0), stop=(i == NI - 1))
            s_te.inc(ins, 1)
            w(t, s_dv, hv)
            for i in range(NI):
                ins = t.matmul(psA[:, 512:640], BA(XL2 + i * 1024 + ts * 128, 128),
                               BA(GWA + i * 128, 128), start=(i == 0), stop=(i == NI - 1))
            s_te.inc(ins, 1)
        T(_)
        g_te = s_te.n

        def _(v, gt=g_te):
            w(v, s_te, gt)
            v.tensor_copy(qsb.ap(), psA[:, 0:512])
            v.tensor_copy(hsb.ap(), psA[:, 512:640])
            v.tensor_tensor(s8b.ap(), qsb.ap()[:, 0:256], qsb.ap()[:, 256:512], ALU.mult)
            for e in range(8):
                ins = v.tensor_reduce(gsml.ap()[:, e:e + 1], s8b.ap()[:, 32 * e:32 * (e + 1)],
                                      AX.X, ALU.add)
            p7["qk_dv"] = s_dv.inc(ins, 1)
        V(_)
        mx_dv = s_dv.n

        def _(a, md=mx_dv):
            w(a, s_dv, md)
            ins = a.activation(gsml.ap()[:, 16:24], gsml.ap()[:, 0:8], AF.Exp,
                               scale=INVSQ_DK, accum_out=gsml.ap()[:, 10:11])
            s_ac.inc(ins, 1)
        A(_)
        ex_ac = s_ac.n

        def _(v, ea=ex_ac):
            w(v, s_ac, ea)
            ins = v.reciprocal(gsml.ap()[:, 11:12], gsml.ap()[:, 10:11])
            s_dv.inc(ins, 1)
        V(_)
        rc_dv = s_dv.n

        def _(a, rd=rc_dv):
            w(a, s_dv, rd)
            ins = a.mul(gsml.ap()[:, 12:13], gsml.ap()[:, 11:12], 1.0)
            s_ac.inc(ins, 1)
        A(_)
        rc_ac = s_ac.n

        def _(v, ra=rc_ac, gb=p7["gb_te"]):
            w(v, s_ac, ra)
            w(v, s_te, gb)
            for e in range(8):
                ins = v.tensor_scalar(gbf.ap()[:, 16 * e:16 * (e + 1)],
                                      hsb.ap()[:, 16 * e:16 * (e + 1)],
                                      gsml.ap()[:, 16 + e:17 + e],
                                      gsml.ap()[:, 12:13], ALU.mult, ALU.mult)
            p7["h_dv"] = s_dv.inc(ins, 1)
        V(_)
        g_dv = s_dv.n

        if debug and ts == 0:
            def _(sy, qa=g_dv, gv=g_dv):
                w(sy, s_dv, qa)
                sy.dma_start(qkd.ap()[:, 0:512], qsb.ap()).then_inc(s_ss.h, 16)
                sy.dma_start(qkd.ap()[:, 512:640], hsb.ap()).then_inc(s_ss.h, 16)
                s_ss.n += 32
                w(sy, s_dv, gv)
                ins = sy.dma_start(gsd.ap(), gsml.ap())
                s_ss.inc(ins, 16)
            SY(_)

        def _(t, gd=g_dv, pa=p7["pt_dv"]):
            w(t, s_dv, gd)
            w(t, s_dv, pa)
            ins = t.transpose(psTb[:, 0:128], gbf.ap(), eyebf.ap())
            p7["gb_te"] = s_te.inc(ins, 1)
        T(_)
        tr_te = s_te.n

        def _(v, ts=ts, tt=tr_te):
            w(v, s_te, tt)
            ins = v.tensor_copy(gt_sb.ap()[:, ts * 128:(ts + 1) * 128], psTb[:, 0:128])
            p7["pt_dv"] = s_dv.inc(ins, 1)
        V(_)
    gt_done = s_dv.n

    if debug:
        def _(sy, gd=gt_done):
            w(sy, s_dv, gd)
            ins = sy.dma_start(gtd.ap(), gt_sb.ap())
            s_ss.inc(ins, 16)
        SY(_)
    fence()

    WFB = [WRK + 34816, WRK + 38912]
    BMT = WRK + 43008
    X2S8 = [0, 512]
    YB = [1024, 1536]

    def _(gp):
        ins = gp.dma_start(BA(BMT, 4096), bmt_i.ap())
        s_gl.inc(ins, 16)
    GP(_)
    bmt_rdy = s_gl.n

    p8 = dict(wf_te=[0, 0], o_dv=[0, 0], ob_ss=[0, 0])
    for o in range(32):
        op = o % 2

        def _(gp, o=o, op=op, wt_=p8["wf_te"][op]):
            w(gp, s_te, wt_)
            ins = gp.dma_start(BA(WFB[op], 4096), wf.ap()[o])
            s_gl.inc(ins, 16)
        GP(_)
        wf_rdy = s_gl.n

        for tc in range(2):
            def _(sy, o=o, tc=tc, xd=p8["o_dv"][tc]):
                w(sy, s_dv, xd)
                ins = sy.dma_start(FA(X2S8[tc], 512),
                                   x2d.ap()[:, o * 1024 + tc * 512: o * 1024 + (tc + 1) * 512])
                p8["x2_sl_%d" % tc] = s_sl.inc(ins, 16)
            SY(_)

            def _(t, o=o, op=op, tc=tc, wr=wf_rdy, br=bmt_rdy, ov=p8["o_dv"][tc]):
                pb = pbE if tc == 0 else pbF
                w(t, s_gl, wr)
                if o == 0:
                    w(t, s_gl, br)
                    w(t, s_dv, gt_done)
                w(t, s_dv, ov)
                for i in range(NI):
                    t.matmul(pb[:, :], BA(WFB[op] + i * 128, 128),
                             BA(XL2 + i * 1024 + tc * 512, 512),
                             start=(i == 0), stop=False)
                ins = t.matmul(pb[:, :], BA(BMT + o * 128, 128),
                               gt_sb.ap()[:, tc * 512:(tc + 1) * 512],
                               start=False, stop=True)
                if tc == 1:
                    p8["wf_te"][op] = s_te.n + 1
                s_te.inc(ins, 1)
            T(_)
            f_te = s_te.n

            def _(v, o=o, tc=tc, ft=f_te, xr=p8["x2_sl_%d" % tc], os_=p8["ob_ss"][tc]):
                pb = pbE if tc == 0 else pbF
                w(v, s_te, ft)
                w(v, s_sl, xr)
                w(v, s_ss, os_)
                v.tensor_tensor(FA(YB[tc], 512), pb[:, :], FA(X2S8[tc], 512), ALU.add)
                ins = v.tensor_scalar(FA(YB[tc], 512), FA(YB[tc], 512),
                                      bcol_sb.ap()[:, o:o + 1], None, ALU.add)
                p8["o_dv"][tc] = s_dv.inc(ins, 1)
            V(_)
            o_dv = s_dv.n

            def _(sy, o=o, tc=tc, od=o_dv):
                w(sy, s_dv, od)
                ins = sy.dma_start(out.ap()[:, o * 1024 + tc * 512: o * 1024 + (tc + 1) * 512],
                                   FA(YB[tc], 512))
                p8["ob_ss"][tc] = s_ss.inc(ins, 16)
            SY(_)

    def _(sy):
        sy.wait_ge(s_ss.h, s_ss.n)
    SY(_)

    block.__exit__(None, None, None)
    return nc



def _pblk(r0, k):
    if k == 0:
        return r0
    if k == r0:
        return 0
    return k


def _phys_pos(r0):
    t = np.arange(S)
    g = t // 512
    blk = (t % 512) // 128
    off = t % 128
    tb = np.array([_pblk(r0, k) for k in range(4)])[blk]
    return g * 512 + tb * 128 + off


def _own_pos(r0):
    u = np.arange(1024)
    return 512 * (u // 128) + 128 * r0 + (u % 128)


def _rope_tables(pos, scale=1.0):
    inv = 1.0 / (10000.0 ** (np.arange(0, 64, dtype=np.float64) * 2.0 / 128.0))
    ang = pos.astype(np.float64)[None, :] * inv[:, None]
    cc = np.concatenate([np.cos(ang), np.cos(ang)], 0) * scale
    ss = np.concatenate([-np.sin(ang), np.sin(ang)], 0) * scale
    return (np.ascontiguousarray(cc.astype(np.float32)),
            np.ascontiguousarray(ss.astype(np.float32)))


def _tiles_T(wm):
    t = wm.reshape(32, 128, 32, 128)
    return np.ascontiguousarray(t.transpose(0, 3, 2, 1).reshape(32, 128, 4096))


def kernel(L0_attn_x, L0_attn_w_in, L0_attn_w_out, L0_ffn_W, L0_ffn_b,
           L0_ffn_Wq, L0_ffn_Wk, L0_ffn_A, L0_ffn_Bm,
           phases=8, debug=False):
    x = np.asarray(L0_attn_x, np.float32)
    w_in = np.asarray(L0_attn_w_in, np.float32)
    w_out = np.asarray(L0_attn_w_out, np.float32)
    W = np.asarray(L0_ffn_W, np.float32)
    b = np.asarray(L0_ffn_b, np.float32)
    Wq = np.asarray(L0_ffn_Wq, np.float32)
    Wk = np.asarray(L0_ffn_Wk, np.float32)
    Af = np.asarray(L0_ffn_A, np.float32).reshape(128, 4096)
    Bm = np.asarray(L0_ffn_Bm, np.float32)

    wq_img = _tiles_T(w_in[0:4096])
    wo_img = _tiles_T(w_out)
    wf_img = _tiles_T(W)
    wk_img = np.zeros((8, 128, 4096), np.float32)
    wv_img = np.zeros((8, 128, 4096), np.float32)
    for h in range(8):
        kw = w_in[4096 + 128 * h:4096 + 128 * (h + 1)]
        vw = w_in[5120 + 128 * h:5120 + 128 * (h + 1)]
        wk_img[h] = kw.reshape(128, 32, 128).transpose(2, 1, 0).reshape(128, 4096)
        wv_img[h] = vw.reshape(128, 32, 128).transpose(2, 1, 0).reshape(128, 4096)

    qk = np.concatenate([Wq, Wk], 0)
    gwqk_img = np.ascontiguousarray(qk.reshape(512, 32, 128).transpose(1, 2, 0))
    gwat_img = np.ascontiguousarray(Af.reshape(128, 32, 128).transpose(1, 2, 0))
    bmt_img = np.ascontiguousarray(Bm.transpose(0, 2, 1).reshape(128, 4096))
    bcol_img = np.ascontiguousarray(b.reshape(32, 128).T)
    eye_img = np.eye(128, dtype=np.float32)

    in_maps = []
    for c in range(8):
        bb, r0 = c // 4, c % 4
        pp = _phys_pos(r0)
        xT = x[bb].T
        xTp = xT[:, pp]
        xt_img = np.ascontiguousarray(
            xTp.reshape(32, 128, 8, 512).transpose(2, 1, 0, 3))

        cck, ssk = _rope_tables(pp)
        ccq, ssq = _rope_tables(_own_pos(r0), scale=1.0 / math.sqrt(128.0))

        kk = np.arange(128)[:, None]
        qq = np.arange(128)[None, :]
        m = np.zeros((128, 4, 4, 128), np.float32)
        for rr in range(4):
            vis = (128 * _pblk(r0, rr) + kk <= 128 * r0 + qq).astype(np.float32)
            for g4 in range(4):
                m[:, rr, g4, :] = vis
        msk_img = np.ascontiguousarray(m.reshape(128, 2048))

        in_maps.append(dict(
            xt=xt_img, wq=wq_img, wk=wk_img, wv=wv_img, wo=wo_img, wf=wf_img,
            gwqk_i=gwqk_img, gwat_i=gwat_img, bmt_i=bmt_img, bcol_i=bcol_img,
            cck_i=cck, ssk_i=ssk, ccq_i=ccq, ssq_i=ssq, msk_i=msk_img,
            eye_i=eye_img,
        ))

    nc = build_nc(phases=phases, debug=debug)
    res = run_bass_kernel_spmd(nc, in_maps, core_ids=list(range(8)))

    if debug:
        return res

    yfull = np.empty((B, S, E), np.float32)
    for c in range(8):
        bb, r0 = c // 4, c % 4
        r = np.asarray(res.results[c]["out"], np.float32)
        yc = r.reshape(128, 32, 1024).transpose(2, 1, 0).reshape(1024, 4096)
        yfull[bb, _own_pos(r0), :] = yc
    return yfull
